# version 06
# baseline (speedup 1.0000x reference)
import numpy as np
import concourse.bass as bass
import concourse.mybir as mybir
from concourse.bass_utils import run_bass_kernel_spmd

F32 = mybir.dt.float32
BF16 = mybir.dt.bfloat16
AF = mybir.ActivationFunctionType
ALU = mybir.AluOpType
AX = mybir.AxisListType

PI = float(np.pi)
MAGIC = 12582912.0
INV2PI = float(np.float64(1.0) / (2.0 * np.pi))
CW1 = float(np.float32(6.28125))
CW2 = float(np.float32(2.0 * np.pi - 6.28125))
CW3 = float(2.0 * np.pi - 6.28125 - float(np.float32(2.0 * np.pi - 6.28125)))
LN1E4_32 = float(np.log(10000.0) / 32.0)
LN0125 = float(np.log(0.125))

N = 8192
D = 1024
NT = 64
NO = 2048
FF = 4096
DEBUG = False
PHASES = 6
SIM_REDUCE = False


def build_program():
    nc = bass.Bass("TRN2", target_bir_lowering=False, debug=False)

    x_d = nc.dram_tensor("x", [N, D], F32, kind="ExternalInput").ap()
    xo_d = nc.dram_tensor("xo", [NO, D], F32, kind="ExternalInput").ap()
    wq_d = nc.dram_tensor("wq", [D, 256], F32, kind="ExternalInput").ap()
    wk_d = nc.dram_tensor("wk", [D, 256], F32, kind="ExternalInput").ap()
    wv_d = nc.dram_tensor("wv", [D, 256], F32, kind="ExternalInput").ap()
    wo_d = nc.dram_tensor("wo", [D, D], F32, kind="ExternalInput").ap()
    bo_d = nc.dram_tensor("bo", [8, 128], F32, kind="ExternalInput").ap()
    wu_d = nc.dram_tensor("wu", [FF, D], F32, kind="ExternalInput").ap()
    wd_d = nc.dram_tensor("wd", [D, FF], F32, kind="ExternalInput").ap()
    out_d = nc.dram_tensor("out", [NO, D], F32, kind="ExternalOutput").ap()

    cc_in = nc.dram_tensor("cc_in", [256, N], BF16, kind="Internal")
    cc_out = nc.dram_tensor("cc_out", [1024, N], BF16, kind="Internal")
    wuT_d = nc.dram_tensor("wuT", [32, 128, D], BF16, kind="Internal").ap()
    wdT_d = nc.dram_tensor("wdT", [8, 128, FF], BF16, kind="Internal").ap()

    dbg = {}
    if DEBUG:
        def dob(nm, sh):
            dbg[nm] = nc.dram_tensor("dbg_" + nm, sh, F32, kind="ExternalOutput").ap()
        dob("q", [128, 2 * N])
        dob("k", [128, 2 * N])
        dob("v", [128, 2 * 160])
        dob("o", [128, 2 * N])
        dob("s", [128, 8 * 512])
        dob("h", [128, 2 * NO])
        dob("m", [128, 8])

    base = (nc.sbuf_base + 31) // 32 * 32
    AVAIL = nc.SBUF_PARTITION_SIZE_BYTES - base

    def at(name, shape, dtype, off):
        assert off % 32 == 0, (name, off)
        return nc.alloc_sbuf_tensor_at(name, shape, dtype, offset=base + off).ap()

    U0, W0 = 0, 131072
    P0 = W0 + 45056
    S0 = P0 + 32768
    TOTAL = S0 + 3840
    assert TOTAL <= AVAIL, (TOTAL, AVAIL)

    q_dm = at("q_dm", [128, 2, N], BF16, U0)
    k_dm = at("k_dm", [128, 2, N], BF16, U0 + 32768)
    v_tm = at("v_tm", [128, NT, 2, 160], BF16, U0 + 65536)
    cos_t = at("cos_t", [128, NT, 32], BF16, U0 + 106496)
    sin_t = at("sin_t", [128, NT, 32], BF16, U0 + 110592)
    afin = at("afin", [1, 1040], F32, U0 + 122880)
    wf_ring = at("wf_ring", [128, 2, 1024], F32, P0 + 0)
    tern_scr = at("tern_scr", [128, 2, 1024], BF16, P0 + 8192)
    wspill = at("wspill", [128, 1024], BF16, P0 + 12288)
    absw_scr = at("absw_scr", [128, 1024], BF16, P0 + 14336)
    o_own = at("o_own", [128, 8, NO], BF16, U0 + 0)
    wout_scr = at("wout_scr", [128, 4, 1024], F32, U0 + 32768)
    h_fm = at("h_fm", [128, 32, NO], BF16, U0 + 0)

    xin = at("xin", [128, 2, 1024], F32, W0 + 0)
    xln = at("xln", [128, 2, 1024], BF16, W0 + 8192)
    xnE = at("xnE", [128, 2, 8, 128], BF16, W0 + 12288)
    qkr = at("qkr", [128, 2, 512], BF16, W0 + 16384)
    wqk = at("wqk", [128, 8, 512], BF16, W0 + 18432)
    wv_s = at("wv_s", [128, 8, 256], BF16, W0 + 26624)
    rp_s = at("rp_s", [128, 4, 256], F32, W0 + 30720)
    th64 = at("th64", [1, 2048], F32, W0 + 34816)
    qkc = at("qkc", [128, 2, 512], BF16, W0 + 34816)
    rp_b = at("rp_b", [128, 2, 2, 256], BF16, W0 + 36864)
    tth = at("tth", [1, 2048], F32, W0 + 0)
    prow = at("prow", [1, 128], F32, W0 + 43008)
    pt_ring = at("pt_ring", [128, 3, 1024], BF16, W0 + 0)
    rcps = at("rcps", [128, 512], F32, W0 + 6144)
    qn_scr = at("qn_scr", [128, 512], BF16, W0 + 8192)
    runmax = at("runmax", [128, 4], F32, W0 + 9216)
    mrow = at("mrow", [128, 16], F32, W0 + 9280)
    wout_bf = at("wout_bf", [128, 8, 1024], BF16, U0 + 65536)
    xn_own = at("xn_own", [128, 8, NO], BF16, W0 + 0)
    gpre = at("gpre", [128, 2, 512], F32, W0 + 0)
    rr2 = at("rr2", [128, 512], F32, W0 + 16384)
    wbufd = at("wbufd", [128, 32, 128], BF16, W0 + 4096)
    xo_ring = at("xo_ring", [128, 2, 1024], F32, W0 + 16384)
    sq_b = at("sq_b", [128, 2048], BF16, W0 + 24576)
    r_bc = at("r_bc", [128, 2048], F32, W0 + 32768)
    wbufu = at("wbufu", [128, 2, 8, 128], BF16, W0 + 40960)
    sq_a = at("sq_a", [128, 2048], BF16, W0 + 40960)
    xin_d = at("xin_d", [128, 2, 1024], F32, U0 + 49152)
    xln_d = at("xln_d", [128, 2, 1024], BF16, U0 + 57344)
    bn_d = at("bn_d", [128, 32], F32, U0 + 61440)
    rr1 = at("rr1", [128, 512], F32, U0 + 61568)

    o_all = at("o_all", [128, 2, N], BF16, P0)
    s_bf = at("s_bf", [128, 8, NO], BF16, P0)

    so = [S0]

    def small(name, shape, dtype):
        nbytes = int(np.prod(shape[1:])) * (2 if dtype == BF16 else 4)
        off = so[0]
        so[0] = (off + nbytes + 31) // 32 * 32
        return at(name, shape, dtype, off)

    identB = small("identB", [128, 128], BF16)
    identF = small("identF", [128, 128], F32)
    onesP = small("onesP", [128, 128], F32)
    pA = small("pA", [128, 128], F32)
    pB = small("pB", [128, 128], F32)
    ind0 = small("ind0", [128, 128], BF16)
    ind64 = small("ind64", [128, 128], BF16)
    mhalf = small("mhalf", [128, 128], BF16)
    msum = small("msum", [128, 1], F32)
    ones1 = small("ones1", [128, 1], BF16)
    mbias = small("mbias", [128, 2], F32)
    epscol = small("epscol", [128, 1], F32)
    l0125 = small("l0125", [128, 1], F32)
    thru = small("thru", [128, 1], F32)
    nthru = small("nthru", [128, 1], F32)
    thrd = small("thrd", [128, 1], F32)
    nthrd = small("nthrd", [128, 1], F32)
    bo_sb = small("bo_sb", [128, 8], F32)
    assert so[0] - S0 <= 4096

    PS = nc.alloc_psum_tensor("PS", [128, 4096], F32).ap()

    def bankb(b, n=1):
        return PS[:, 512 * b:512 * (b + n)].bitcast(BF16)

    s_sp = nc.alloc_semaphore("s_sp")
    s_gp = nc.alloc_semaphore("s_gp")
    s_pe = nc.alloc_semaphore("s_pe")
    s_act = nc.alloc_semaphore("s_act")
    s_dve = nc.alloc_semaphore("s_dve")
    s_pool = nc.alloc_semaphore("s_pool")
    s_cc = nc.alloc_semaphore("s_cc")
    sx = [nc.alloc_semaphore("sx0"), nc.alloc_semaphore("sx1")]
    sw = [nc.alloc_semaphore("sw0"), nc.alloc_semaphore("sw1")]
    ssp1 = nc.alloc_semaphore("ssp1")
    su = [nc.alloc_semaphore("su0"), nc.alloc_semaphore("su1")]
    sdd = nc.alloc_semaphore("sdd")
    so = [nc.alloc_semaphore("so0"), nc.alloc_semaphore("so1")]
    cnt = {s_sp: 0, s_gp: 0, s_pe: 0, s_act: 0, s_dve: 0, s_pool: 0, s_cc: 0,
           ssp1: 0, sdd: 0}
    for _s in sx + sw + su + so:
        cnt[_s] = 0

    def inc(inst, sem, d=1):
        inst.then_inc(sem, d)
        cnt[sem] += d
        return (sem, cnt[sem])

    def wt(eng, tok):
        if tok is not None:
            eng.wait_ge(tok[0], tok[1])

    SP, PE, ACT, DVE, GP = nc.sync, nc.tensor, nc.scalar, nc.vector, nc.gpsimd

    def spdma(dst, src, sem=None):
        return inc(SP.dma_start(out=dst, in_=src), sem or s_sp, 16)

    def gpdma(dst, src, sem=None):
        return inc(GP.dma_start(out=dst, in_=src), sem or s_gp, 16)

    GP.iota(prow[:, :], pattern=[[1, 128]], base=0, channel_multiplier=0,
            allow_small_or_imprecise_dtypes=True)
    GP.iota(th64[:, :], pattern=[[0, 64], [1, 32]], base=0, channel_multiplier=0,
            allow_small_or_imprecise_dtypes=True)
    GP.iota(tth[:, :], pattern=[[128, 64], [0, 32]], base=0, channel_multiplier=0,
            allow_small_or_imprecise_dtypes=True)
    def gpop(inst):
        tok = inc(inst, s_pool)
        GP.wait_ge(tok[0], tok[1])
        return tok
    for ap_ in (identB, identF):
        gpop(GP.memset(ap_[:, :], 0.0))
        gpop(GP.affine_select(out=ap_[:, :], in_=ap_[:, :], compare_op=ALU.not_equal,
                              fill=1.0, base=0, pattern=[[-1, 128]], channel_multiplier=1))
    gpop(GP.memset(onesP[:, :], 0.0))
    gpop(GP.memset(onesP[0:1, :], 1.0))
    gpop(GP.memset(onesP[64:65, :], 1.0))
    gpop(GP.memset(pA[:, :], 0.0))
    gpop(GP.memset(pA[64:65, 0:64], 1.0))
    gpop(GP.memset(pB[:, :], 0.0))
    gpop(GP.memset(pB[32:33, 64:128], 1.0))
    gpop(GP.memset(ind0[:, :], 0.0))
    gpop(GP.memset(ind0[:, 0:1], 1.0))
    gpop(GP.memset(ind64[:, :], 0.0))
    gpop(GP.memset(ind64[:, 64:65], 1.0))
    gpop(GP.memset(mhalf[:, :], 0.0))
    gpop(GP.memset(mhalf[0:64, 0:1], 1.0))
    gpop(GP.memset(mhalf[64:128, 64:65], 1.0))
    gpop(GP.memset(msum[:, :], 0.0))
    gpop(GP.memset(msum[0:1, :], 1.0))
    gpop(GP.memset(msum[64:65, :], 1.0))
    gpop(GP.memset(ones1[:, :], 1.0))
    gpop(GP.memset(epscol[:, :], 1e-5))
    gpop(GP.memset(l0125[:, :], LN0125))
    gpop(GP.memset(v_tm[:, :, :, :], 0.0))
    t_cst = gpop(GP.memset(v_tm[:, :, :, 64:65], 1.0))

    wt(ACT, t_cst)
    t_th = inc(ACT.activation(th64[:, :], th64[:, :], AF.Exp, scale=-LN1E4_32), s_act)
    wt(DVE, t_th)
    t_tth = inc(DVE.tensor_mul(tth[:, :], tth[:, :], th64[:, :]), s_dve)
    t_tabs = None
    for b in range(4):
        tb = PS[:, 2560:3072]
        red = rp_s[:, 0:2, :].rearrange("p a d -> p (a d)")
        scr = rp_s[:, 2:4, :].rearrange("p a d -> p (a d)")
        wt(PE, t_tth)
        if b > 0:
            wt(PE, (s_act, cnt[s_act]))
        PE.matmul(tb, prow[:, :], th64[:, 0:512], start=True, stop=False)
        t_ang = inc(PE.matmul(tb, onesP[0:1, :], tth[:, 512 * b:512 * (b + 1)],
                              start=False, stop=True), s_pe)
        wt(DVE, t_ang)
        t = inc(DVE.tensor_scalar(scr, tb, INV2PI, MAGIC, op0=ALU.mult, op1=ALU.add), s_dve)
        wt(DVE, t)
        t = inc(DVE.tensor_scalar_add(scr, scr, -MAGIC), s_dve)
        wt(DVE, t)
        t = inc(DVE.scalar_tensor_tensor(red, scr, -CW1, tb, op0=ALU.mult, op1=ALU.add), s_dve)
        wt(DVE, t)
        t = inc(DVE.scalar_tensor_tensor(red, scr, -CW2, red, op0=ALU.mult, op1=ALU.add), s_dve)
        wt(DVE, t)
        t = inc(DVE.scalar_tensor_tensor(red, scr, -CW3, red, op0=ALU.mult, op1=ALU.add), s_dve)
        wt(DVE, t)
        t = inc(DVE.tensor_scalar_min(red, red, PI), s_dve)
        wt(DVE, t)
        tred = inc(DVE.tensor_scalar_max(red, red, -PI), s_dve)
        wt(ACT, tred)
        tsin = inc(ACT.activation(
            sin_t[:, 16 * b:16 * (b + 1), :].rearrange("p a d -> p (a d)"),
            red, AF.Sin), s_act)
        wt(DVE, tred)
        t = inc(DVE.tensor_scalar_add(scr, red, PI / 2), s_dve)
        wt(DVE, t)
        wt(DVE, tsin)
        t = inc(DVE.tensor_single_scalar(red, scr, PI, op=ALU.is_gt), s_dve)
        wt(DVE, t)
        t = inc(DVE.scalar_tensor_tensor(scr, red, -2 * PI, scr, op0=ALU.mult, op1=ALU.add), s_dve)
        wt(ACT, t)
        wt(ACT, tsin)
        t_tabs = inc(ACT.activation(
            cos_t[:, 16 * b:16 * (b + 1), :].rearrange("p a d -> p (a d)"),
            scr, AF.Sin), s_act)

    def load_w256(dst_slice, src):
        toks = []
        for hh in range(2):
            tl = spdma(xin[:, hh, :].rearrange("p (c n) -> p c n", c=4),
                       src.rearrange("(c p) n -> p c n", p=128)[:, 4 * hh:4 * (hh + 1), :],
                       sem=sx[hh])
            toks.append(tl)
        wt(DVE, toks[0])
        wt(DVE, toks[1])
        t0 = inc(DVE.tensor_copy(dst_slice(0), xin[:, 0, :].rearrange("p (c n) -> p c n", c=4)), s_dve)
        t1 = inc(DVE.tensor_copy(dst_slice(1), xin[:, 1, :].rearrange("p (c n) -> p c n", c=4)), s_dve)
        wt(SP, t1)
        return t1

    wt(SP, (s_pe, cnt[s_pe]))
    wt(SP, t_tabs)
    load_w256(lambda hh: wqk[:, 4 * hh:4 * (hh + 1), 0:256], wq_d)
    t_wq = load_w256(lambda hh: wqk[:, 4 * hh:4 * (hh + 1), 256:512], wk_d)
    t_wv = load_w256(lambda hh: wv_s[:, 4 * hh:4 * (hh + 1), :], wv_d)

    wt(GP, (s_dve, cnt[s_dve]))
    t_bo = gpdma(rp_s[0:8, 0, 0:128], bo_d[:, :])
    wt(PE, t_bo)
    wt(PE, t_tabs)
    t = inc(PE.transpose(PS[0:128, 2560:2568], rp_s[0:8, 0, 0:128], identF[0:8, 0:8]), s_pe)
    wt(DVE, t)
    t_bosb = inc(DVE.tensor_copy(bo_sb[:, :], PS[:, 2560:2568]), s_dve)

    def blocks_up():
        return [(wu_d[128 * i:128 * (i + 1), :], wuT_d[i, :, :]) for i in range(32)]

    def blocks_dn():
        out = []
        for dcq in range(32):
            dc, q4 = dcq // 4, dcq % 4
            out.append((wd_d[128 * dc:128 * (dc + 1), 1024 * q4:1024 * (q4 + 1)],
                        wdT_d[dc, :, 1024 * q4:1024 * (q4 + 1)]))
        return out

    AB7 = PS[0:1, 3584:4096]

    def alpha_jobs(blocks, state, dep):
        nb = len(blocks)
        for i, (src, _) in enumerate(blocks):
            def load(i=i, src=src):
                sl = i % 2
                if state.get("abs%d" % sl) is not None:
                    wt(GP, state["abs%d" % sl])
                elif state.get("war") is not None:
                    wt(GP, state["war"])
                state["ld%d" % sl] = gpdma(wf_ring[:, sl, :], src, sem=sw[sl])

            def comp(i=i):
                sl = i % 2
                if state.get("mm") is not None:
                    wt(ACT, state["mm"])
                wt(ACT, state["ld%d" % sl])
                wt(ACT, dep)
                if i == 0 and state.get("war") is not None:
                    wt(ACT, state["war"])
                state["abs%d" % sl] = state["abs"] = inc(
                    ACT.activation(absw_scr[:, :], wf_ring[:, sl, :], AF.Abs), s_act)
                wt(PE, state["abs"])
                if i == 0 and state.get("war") is not None:
                    wt(PE, state["war"])
                PE.matmul(AB7, ones1[:, :], absw_scr[:, 512:1024],
                          start=(i == 0), stop=False)
                state["mm"] = inc(PE.matmul(AB7, ones1[:, :], absw_scr[:, 0:512],
                                            start=False, stop=(i == nb - 1)), s_pe)
            yield (load, comp)

    def alpha_fin(state, nb, thr_ap, nthr_ap):
        tok_mm = state["mm"]
        wt(DVE, tok_mm)
        t = inc(DVE.tensor_copy(afin[0:1, 0:512], AB7[0:1, 0:512]), s_dve)
        wt(DVE, t)
        t = inc(DVE.reduce_sum(afin[0:1, 1024:1025], afin[0:1, 0:512], axis=AX.X),
                s_dve)
        wt(DVE, t)
        t = inc(DVE.tensor_scalar_mul(afin[0:1, 1024:1025], afin[0:1, 1024:1025],
                                      0.5 / (nb * 128 * 1024)), s_dve)
        wt(PE, t)
        wt(PE, tok_mm)
        t = inc(PE.matmul(PS[0:128, 3584:3585], onesP[0:1, :], afin[0:1, 1024:1025],
                          start=True, stop=True), s_pe)
        wt(DVE, t)
        t = inc(DVE.tensor_copy(thr_ap[:, :], PS[0:128, 3584:3585]), s_dve)
        wt(DVE, t)
        t = inc(DVE.tensor_scalar_mul(nthr_ap[:, :], thr_ap[:, :], -1.0), s_dve)
        state["thr"] = t
        return t

    def spill_jobs(blocks, state):
        for i, (src, dst, thr_ap, nthr_ap, dep_key) in enumerate(blocks):
            def load(i=i, src=src, dep_key=dep_key):
                sl = i % 2
                if state.get("tern%d" % sl) is not None:
                    wt(GP, state["tern%d" % sl])
                if i < 2:
                    wt(GP, state[dep_key])
                state["ld%d" % sl] = gpdma(wf_ring[:, sl, :], src, sem=sw[sl])

            def comp(i=i, dst=dst, thr_ap=thr_ap, nthr_ap=nthr_ap,
                     dep_key=dep_key):
                sl = i % 2
                wt(DVE, state["ld%d" % sl])
                wt(DVE, state[dep_key])
                if state.get("spill") is not None:
                    wt(DVE, state["spill"])
                t = inc(DVE.tensor_scalar(tern_scr[:, 0, :], wf_ring[:, sl, :],
                                          thr_ap[:, :], None, op0=ALU.is_gt),
                        s_dve)
                wt(DVE, t)
                t = inc(DVE.tensor_scalar(tern_scr[:, 1, :], wf_ring[:, sl, :],
                                          nthr_ap[:, :], None, op0=ALU.is_lt),
                        s_dve)
                wt(DVE, t)
                t_wt = inc(DVE.tensor_sub(tern_scr[:, 0, :], tern_scr[:, 0, :],
                                          tern_scr[:, 1, :]), s_dve)
                state["tern%d" % sl] = state["tern"] = t_wt
                pb = bankb(6)
                wt(PE, t_wt)
                if state.get("ev") is not None:
                    wt(PE, state["ev"])
                ttp = None
                for ci in range(8):
                    ttp = inc(PE.transpose(pb[:, 128 * ci:128 * (ci + 1)],
                                           tern_scr[:, 0, 128 * ci:128 * (ci + 1)],
                                           identB[:, :]), s_pe)
                wt(DVE, ttp)
                tok_ev = inc(DVE.tensor_copy(wspill[:, :], pb[:, :]), s_dve)
                state["ev"] = tok_ev
                wt(GP, tok_ev)
                state["spill"] = gpdma(dst, wspill[:, :], sem=ssp1)
            yield (load, comp)

    st_au = {}
    st_ad = {}

    st_sp = {}

    def fin_up_job():
        alpha_fin(st_au, 32, thru, nthru)
        st_sp["thr_up"] = st_au["thr"]
        st_ad["war"] = st_au["thr"]

    def fin_dn_job():
        alpha_fin(st_ad, 32, thrd, nthrd)
        st_sp["thr_dn"] = st_ad["thr"]

    spill_blocks = ([(s, d, thru, nthru, "thr_dn") for (s, d) in blocks_up()] +
                    [(s, d, thrd, nthrd, "thr_dn") for (s, d) in blocks_dn()])
    wjobs = (list(alpha_jobs(blocks_up(), st_au, t_bosb)) + [(None, fin_up_job)] +
             list(alpha_jobs(blocks_dn(), st_ad, t_bosb)) + [(None, fin_dn_job)] +
             list(spill_jobs(spill_blocks, st_sp)))
    wjob_i = [0]
    wload_i = [0]

    def run_wjobs(target):
        tgt = min(target, len(wjobs))
        while wjob_i[0] < tgt:
            while wload_i[0] < min(wjob_i[0] + 2, len(wjobs)):
                ld = wjobs[wload_i[0]][0]
                if ld is None:
                    if wload_i[0] >= wjob_i[0]:
                        break
                    wload_i[0] += 1
                    continue
                ld()
                wload_i[0] += 1
            wjobs[wjob_i[0]][1]()
            wjob_i[0] += 1

    SP.wait_ge(sx[0], cnt[sx[0]])
    SP.wait_ge(sx[1], cnt[sx[1]])
    nc.all_engine_barrier()
    if PHASES <= 1:
        return nc, dbg

    def emit_ln_tile(src_rows, sl, dep_war_xin, dep_war_xln, xin_ap, xln_ap, bnp,
                     t_x):
        bn_ap = bnp[:, 16 * sl:16 * (sl + 1)]
        wt(DVE, t_x)
        t1 = inc(DVE.bn_stats(bn_ap[:, 0:6], xin_ap[:, sl, 0:512]), s_dve)
        t2 = inc(DVE.bn_stats(bn_ap[:, 6:12], xin_ap[:, sl, 512:1024]), s_dve)
        wt(DVE, t2)
        t3 = inc(DVE.bn_aggr(bn_ap[:, 12:14], bn_ap[:, 0:12]), s_dve)
        wt(ACT, t3)
        t4 = inc(ACT.activation(bn_ap[:, 14:15], bn_ap[:, 13:14], AF.Ln,
                                bias=epscol[:, :]), s_act)
        wt(ACT, t4)
        t5 = inc(ACT.activation(bn_ap[:, 14:15], bn_ap[:, 14:15], AF.Exp,
                                scale=-0.5), s_act)
        wt(DVE, t5)
        t5b = inc(DVE.scalar_tensor_tensor(bn_ap[:, 15:16], bn_ap[:, 12:13], -1.0,
                                           bn_ap[:, 14:15], op0=ALU.mult,
                                           op1=ALU.mult), s_dve)
        wt(ACT, t5b)
        if dep_war_xln is not None:
            wt(ACT, dep_war_xln)
        t6 = inc(ACT.activation(xln_ap[:, sl, :], xin_ap[:, sl, :], AF.Identity,
                                bias=bn_ap[:, 15:16], scale=bn_ap[:, 14:15]),
                 s_act)
        return t6

    def ln_dma(src_rows, sl, dep_war_xin, xin_ap):
        if dep_war_xin is not None:
            wt(SP, dep_war_xin)
        return spdma(xin_ap[:, sl, :], src_rows, sem=sx[sl])

    bn_a = rp_s[:, 0, 0:32]

    a_xin_war = [None, None]
    a_xln_war = [None, None]
    tok_xnev_last = [None]
    a_xnE_war = [None, None]
    tok_rope = [None, None]
    a_qkr_war = [None, None]
    a_qkc_war = [None, None]
    tok_vev_prev = None
    tok_qkev_prev = None

    a_xdma = [None, None]
    a_xdma[0] = ln_dma(x_d[0:128, :], 0, None, xin)
    VBANK = [PS[:, 1024:1280], PS[:, 2560:2816]]
    a_vev = [None, None]
    a_tb_war = [None, None]
    for tix in range(NT):
        if tix + 1 < NT:
            slp = (tix + 1) % 2
            a_xdma[slp] = ln_dma(x_d[128 * (tix + 1):128 * (tix + 2), :], slp,
                                 a_xin_war[slp], xin)
        run_wjobs((tix + 1) * 131 // 64)
        sl = tix % 2
        t_xln = emit_ln_tile(x_d[128 * tix:128 * (tix + 1), :], sl,
                             a_xin_war[sl], a_xln_war[sl], xin, xln, bn_a,
                             a_xdma[sl])
        a_xin_war[sl] = t_xln
        pbx = bankb(3 if sl == 0 else 4)
        wt(PE, t_xln)
        if a_tb_war[sl] is not None:
            wt(PE, a_tb_war[sl])
        ttp = None
        for c in range(8):
            ttp = inc(PE.transpose(pbx[:, 128 * c:128 * (c + 1)],
                                   xln[:, sl, 128 * c:128 * (c + 1)],
                                   identB[:, :]), s_pe)
        a_xln_war[sl] = ttp
        wt(DVE, ttp)
        if a_xnE_war[sl] is not None:
            wt(DVE, a_xnE_war[sl])
        t_xnE = inc(DVE.tensor_copy(xnE[:, sl, :, :],
                                    pbx[:, :].rearrange("p (c n) -> p c n", c=8)),
                    s_dve)
        wt(PE, t_xnE)
        wt(PE, t_wv)
        if a_vev[sl] is not None:
            wt(PE, a_vev[sl])
        tvm = None
        for c in range(8):
            tvm = inc(PE.matmul(VBANK[sl], xnE[:, sl, c, :], wv_s[:, c, :],
                                start=(c == 0), stop=(c == 7)), s_pe)
        qb = sl
        wt(PE, t_wq)
        if tok_rope[qb] is not None:
            wt(PE, tok_rope[qb])
        tqk = None
        for c in range(8):
            tqk = inc(PE.matmul(PS[:, 512 * qb:512 * (qb + 1)], xnE[:, sl, c, :],
                                wqk[:, c, :], start=(c == 0), stop=(c == 7)), s_pe)
        a_xnE_war[sl] = tqk
        wt(ACT, tvm)
        vdst = bass.AP(v_tm.tensor, tix * 320,
                       [[20480, 128], [160, 2], [96, 2], [1, 64]])
        a_vev[sl] = inc(ACT.activation(
            vdst, VBANK[sl].rearrange("p (a h d) -> p a h d", a=2, h=2),
            AF.Identity), s_act)
        qkp = PS[:, 512 * qb:512 * (qb + 1)]
        wt(ACT, tqk)
        wt(ACT, t_tabs)
        if a_qkc_war[sl] is not None:
            wt(ACT, a_qkc_war[sl])
        t_qkc = inc(ACT.activation(qkc[:, sl, :], qkp, AF.Identity), s_act)
        qv = qkc[:, sl, :].rearrange("p (h d) -> p h d", d=64)
        x1 = qv[:, :, 0:32]
        x2 = qv[:, :, 32:64]
        cosv = cos_t[:, tix, :].unsqueeze(1).broadcast_to([128, 8, 32])
        sinv = sin_t[:, tix, :].unsqueeze(1).broadcast_to([128, 8, 32])
        r3 = rp_b[:, sl, 0, :].rearrange("p (a d) -> p a d", d=32)[:, 0:8, :]
        r4 = rp_b[:, sl, 1, :].rearrange("p (a d) -> p a d", d=32)[:, 0:8, :]
        o1 = qkr[:, sl, :].rearrange("p (h d) -> p h d", d=64)[:, :, 0:32]
        o2 = qkr[:, sl, :].rearrange("p (h d) -> p h d", d=64)[:, :, 32:64]
        wt(DVE, t_qkc)
        wt(DVE, t_tabs)
        if a_qkr_war[sl] is not None:
            wt(DVE, a_qkr_war[sl])
        DVE.tensor_tensor(r3, x1, cosv, op=ALU.mult)
        tr = inc(DVE.tensor_tensor(r4, x2, sinv, op=ALU.mult), s_dve)
        wt(DVE, tr)
        tr = inc(DVE.tensor_sub(o1, r3, r4), s_dve)
        wt(DVE, tr)
        DVE.tensor_tensor(r3, x1, sinv, op=ALU.mult)
        tr = inc(DVE.tensor_tensor(r4, x2, cosv, op=ALU.mult), s_dve)
        wt(DVE, tr)
        t_rope = inc(DVE.tensor_add(o2, r3, r4), s_dve)
        tok_rope[qb] = t_rope
        a_qkc_war[sl] = t_rope
        pbq = pbx
        wt(PE, t_rope)
        wt(PE, t_xnE)
        ttq = None
        for c in range(4):
            ttq = inc(PE.transpose(pbq[:, 128 * c:128 * (c + 1)],
                                   qkr[:, sl, 128 * c:128 * (c + 1)],
                                   identB[:, :]), s_pe)
        a_qkr_war[sl] = ttq
        wt(DVE, ttq)
        DVE.tensor_copy(q_dm[:, :, 128 * tix:128 * (tix + 1)],
                        pbq[:, 0:256].rearrange("p (a n) -> p a n", a=2))
        a_tb_war[sl] = inc(
            DVE.tensor_copy(k_dm[:, :, 128 * tix:128 * (tix + 1)],
                            pbq[:, 256:512].rearrange("p (a n) -> p a n", a=2)),
            s_dve)

    run_wjobs(len(wjobs))
    GP.wait_ge(ssp1, cnt[ssp1])
    GP.wait_ge(sw[0], cnt[sw[0]])
    GP.wait_ge(sw[1], cnt[sw[1]])
    GP.wait_ge(s_gp, cnt[s_gp])
    nc.all_engine_barrier()

    if DEBUG:
        gpdma(dbg["q"].rearrange("p (a n) -> p a n", a=2), q_dm[:, :, :])
        gpdma(dbg["k"].rearrange("p (a n) -> p a n", a=2), k_dm[:, :, :])
        gpdma(dbg["v"].rearrange("p (a n) -> p a n", a=2), v_tm[:, 0, :, :])
        GP.wait_ge(s_gp, cnt[s_gp])
        nc.all_engine_barrier()
    if PHASES <= 2:
        return nc, dbg

    tok_run = inc(DVE.memset(runmax[:, :], 0.0), s_dve)
    tok_rmax = None
    for j, src in ((0, q_dm), (1, k_dm)):
        for pr in range(2):
            for g in range(16):
                sl512 = src[:, pr, 512 * g:512 * (g + 1)]
                wt(DVE, tok_rmax)
                tsq = inc(DVE.tensor_mul(qn_scr[:, :], sl512, sl512), s_dve)
                wt(PE, tsq)
                if tok_rmax is not None:
                    wt(PE, tok_rmax)
                tmm = inc(PE.matmul(PS[:, 1536:2048], mhalf[:, :], qn_scr[:, :],
                                    start=True, stop=True), s_pe)
                wt(DVE, tmm)
                tr1 = inc(DVE.reduce_max(rp_s[:, 3, 0:1], PS[:, 1536:2048], axis=AX.X),
                          s_dve)
                wt(DVE, tr1)
                wt(DVE, tok_run)
                ji = 2 * j + pr
                tok_run = inc(DVE.tensor_max(runmax[:, ji:ji + 1],
                                             runmax[:, ji:ji + 1],
                                             rp_s[:, 3, 0:1]), s_dve)
                tok_rmax = tok_run
    wt(DVE, tok_run)
    t = inc(DVE.tensor_mul(runmax[:, 0:2], runmax[:, 0:2], runmax[:, 2:4]), s_dve)
    wt(PE, t)
    t = inc(PE.matmul(PS[0:1, 1536:1538], msum[:, :], runmax[:, 0:2],
                      start=True, stop=True), s_pe)
    wt(DVE, t)
    t = inc(DVE.tensor_copy(mrow[0:1, 0:2], PS[0:1, 1536:1538]), s_dve)
    wt(ACT, t)
    t = inc(ACT.activation(mrow[0:1, 2:4], mrow[0:1, 0:2], AF.Ln), s_act)
    wt(ACT, t)
    t = inc(ACT.activation(mrow[0:1, 4:6], mrow[0:1, 2:4], AF.Exp, scale=0.5,
                           bias=l0125[0:1, :]), s_act)
    wt(DVE, t)
    t = inc(DVE.tensor_scalar_mul(mrow[0:1, 6:8], mrow[0:1, 4:6], -1.0), s_dve)
    wt(PE, t)
    t = inc(PE.matmul(PS[:, 1536:1538], onesP[0:1, :], mrow[0:1, 6:8],
                      start=True, stop=True), s_pe)
    wt(DVE, t)
    t_mb = inc(DVE.tensor_copy(mbias[:, :], PS[:, 1536:1538]), s_dve)

    nc.all_engine_barrier()
    if PHASES <= 3:
        return nc, dbg

    if SIM_REDUCE:
        GP.memset(o_all[:, :, :], 0.0)
        nc.all_engine_barrier()
    NQ = 16
    NKT = 64
    b_pairs = [0] if PHASES in (3.5, 3.7) else range(2)
    b_qgs = [0] if PHASES == 3.5 else range(NQ)
    if SIM_REDUCE:
        b_pairs = range(2)
        b_qgs = [0, 1]
    tok_exp = {}
    tok_pv = {}
    tok_sc = {}
    stage_extra = {}
    prev_norm_tok = None

    def emit_pv(m_, pr_, first, last):
        kt_ = m_ % NKT
        stp = m_ % 3
        wt(PE, tok_exp[m_])
        if first and prev_norm_tok is not None:
            wt(PE, prev_norm_tok)
        PE.matmul(PS[0:65, 3072:3584], v_tm[:, kt_, pr_, 0:65],
                  pt_ring[:, stp, 0:512], start=first, stop=last)
        tok_pv[m_] = inc(
            PE.matmul(PS[:, 3584:4096], v_tm[:, kt_, pr_, 32:160],
                      pt_ring[:, stp, 512:1024], start=first, stop=last), s_pe)

    for pr in b_pairs:
        for qg in b_qgs:
            qslice = slice(512 * qg, 512 * (qg + 1))
            for kt in range(NKT):
                m = (pr * NQ + qg) * NKT + kt
                st = m % 3
                sb_ = PS[:, 1024 * st:1024 * (st + 1)]
                if (m - 3) in tok_exp:
                    wt(PE, tok_exp[m - 3])
                if st in stage_extra:
                    wt(PE, stage_extra.pop(st))
                PE.matmul(sb_[:, 0:512], k_dm[0:64, pr, 128 * kt:128 * (kt + 1)],
                          q_dm[0:64, pr, qslice], start=True, stop=True)
                tok_sc[m] = inc(
                    PE.matmul(sb_[:, 512:1024],
                              k_dm[64:128, pr, 128 * kt:128 * (kt + 1)],
                              q_dm[64:128, pr, qslice], start=True, stop=True), s_pe)
                wt(ACT, tok_sc[m])
                tok_exp[m] = inc(
                    ACT.activation(pt_ring[:, st, :], sb_[:, :], AF.Exp,
                                   bias=mbias[:, pr:pr + 1], scale=0.125), s_act)
                if kt > 1:
                    emit_pv(m - 2, pr, kt - 2 == 0, False)
            m_last = (pr * NQ + qg) * NKT + NKT - 1
            emit_pv(m_last - 1, pr, False, False)
            emit_pv(m_last, pr, False, True)
            wt(ACT, tok_pv[m_last])
            ta = inc(ACT.activation(rcps[64:65, :], PS[64:65, 3072:3584], AF.Ln), s_act)
            wt(ACT, ta)
            ta = inc(ACT.activation(rcps[64:65, :], rcps[64:65, :], AF.Exp,
                                    scale=-1.0), s_act)
            tb_ = inc(ACT.activation(rcps[32:33, :], PS[32:33, 3584:4096], AF.Ln), s_act)
            wt(ACT, tb_)
            tb_ = inc(ACT.activation(rcps[32:33, :], rcps[32:33, :], AF.Exp,
                                     scale=-1.0), s_act)
            rst = (m_last + 2) % 3
            rbankA = PS[:, 1024 * rst:1024 * rst + 512]
            rbankB = PS[:, 1024 * rst + 512:1024 * rst + 1024]
            wt(PE, ta)
            wt(PE, tb_)
            PE.matmul(rbankA, pA[64:65, :], rcps[64:65, :], start=True, stop=True)
            tR = inc(PE.matmul(rbankB, pB[32:33, :], rcps[32:33, :],
                               start=True, stop=True), s_pe)
            wt(DVE, tR)
            DVE.tensor_copy(rcps[0:64, :], rbankA[0:64, :])
            tc = inc(DVE.tensor_copy(rcps[64:128, :], rbankB[64:128, :]), s_dve)
            wt(DVE, tc)
            DVE.tensor_tensor(o_all[0:64, pr, qslice], PS[0:64, 3072:3584],
                              rcps[0:64, :], op=ALU.mult)
            prev_norm_tok = inc(
                DVE.tensor_tensor(o_all[64:128, pr, qslice], PS[64:128, 3584:4096],
                                  rcps[64:128, :], op=ALU.mult), s_dve)
            stage_extra[rst] = tc

    nc.all_engine_barrier()

    if DEBUG:
        gpdma(dbg["o"].rearrange("p (a n) -> p a n", a=2), o_all[:, :, :])
        GP.wait_ge(s_gp, cnt[s_gp])
        nc.all_engine_barrier()
    if PHASES <= 4.5:
        return nc, dbg

    def rms_to_R(get_chunk, nchunk, inv_d, sq_ap, rr_ap, dep=None):
        tok_mm = None
        tsq_prev = dep
        first = {4: True, 5: True}
        lastmm = {}
        for ci in range(nchunk):
            if tsq_prev is not None:
                wt(DVE, tsq_prev)
            tsq = inc(DVE.tensor_mul(sq_ap[:, :], get_chunk(ci), get_chunk(ci)), s_dve)
            wt(PE, tsq)
            for g in range(4):
                bk = PS[:, 512 * (4 + g // 2):512 * (5 + g // 2)]
                ind = ind0 if g % 2 == 0 else ind64
                b_ = 4 + g // 2
                tok_mm = inc(PE.matmul(bk, ind[:, :], sq_ap[:, 512 * g:512 * (g + 1)],
                                       start=first[b_],
                                       stop=(ci == nchunk - 1 and g % 2 == 1)), s_pe)
                first[b_] = False
            tsq_prev = tok_mm
        tlast = None
        for g in range(4):
            bk = PS[:, 512 * (4 + g // 2):512 * (5 + g // 2)]
            rr = 64 * (g % 2)
            wt(ACT, tok_mm)
            if tlast is not None:
                pass
            ta = inc(ACT.activation(rr_ap[rr:rr + 1, :], bk[rr:rr + 1, :], AF.Ln,
                                    scale=inv_d), s_act)
            wt(ACT, ta)
            ta = inc(ACT.activation(rr_ap[rr:rr + 1, :], rr_ap[rr:rr + 1, :], AF.Exp,
                                    scale=0.5), s_act)
            wt(DVE, ta)
            td = inc(DVE.tensor_scalar_add(rr_ap[rr:rr + 1, :], rr_ap[rr:rr + 1, :],
                                           1e-8), s_dve)
            wt(ACT, td)
            ta = inc(ACT.activation(rr_ap[rr:rr + 1, :], rr_ap[rr:rr + 1, :], AF.Ln), s_act)
            wt(ACT, ta)
            ta = inc(ACT.activation(rr_ap[rr:rr + 1, :], rr_ap[rr:rr + 1, :], AF.Exp,
                                    scale=-1.0), s_act)
            wt(PE, ta)
            if tlast is not None:
                wt(PE, tlast)
            tp = inc(PE.matmul(PS[:, 3072:3584], onesP[rr:rr + 1, :],
                               rr_ap[rr:rr + 1, :], start=True, stop=True), s_pe)
            wt(DVE, tp)
            tlast = inc(DVE.tensor_copy(r_bc[:, 512 * g:512 * (g + 1)],
                                        PS[:, 3072:3584]), s_dve)
        return tlast


    rloc = nc.partition_id() % 4
    tgi = gpdma(cc_in.ap().rearrange("(a p) t -> p a t", p=128), o_all[:, :, :])
    wt(GP, tgi)
    if SIM_REDUCE:
        for r4 in range(4):
            tcc = gpdma(cc_out.ap()[256 * r4:256 * (r4 + 1), :], cc_in.ap())
        GP.wait_ge(s_gp, cnt[s_gp])
    else:
        tcc = inc(GP.collective_compute(
            "AllGather", ALU.bypass,
            replica_groups=[[0, 1, 2, 3], [4, 5, 6, 7]],
            ins=[cc_in.ap().opt()], outs=[cc_out.ap().opt()]), s_cc)
    d_xin_war = [None, None]
    d_xln_war = [None, None]
    tok_xnev2 = [None]
    d_xdma = [None, None]
    d_xdma[0] = ln_dma(xo_d[0:128, :], 0, None, xin_d)
    for tix in range(16):
        if tix + 1 < 16:
            slp = (tix + 1) % 2
            d_xdma[slp] = ln_dma(xo_d[128 * (tix + 1):128 * (tix + 2), :], slp,
                                 d_xin_war[slp], xin_d)
        sl = tix % 2
        t_xln = emit_ln_tile(xo_d[128 * tix:128 * (tix + 1), :], sl,
                             d_xin_war[sl], d_xln_war[sl], xin_d, xln_d,
                             bn_d[:, 0:32].rearrange("p n -> p n"), d_xdma[sl])
        d_xin_war[sl] = t_xln
        pbx = bankb(3)
        wt(PE, t_xln)
        if tok_xnev2[0] is not None:
            wt(PE, tok_xnev2[0])
        ttp = None
        for c in range(8):
            ttp = inc(PE.transpose(pbx[:, 128 * c:128 * (c + 1)],
                                   xln_d[:, sl, 128 * c:128 * (c + 1)],
                                   identB[:, :]), s_pe)
        d_xln_war[sl] = ttp
        wt(DVE, ttp)
        tok_xnev2[0] = inc(DVE.tensor_copy(
            xn_own[:, :, 128 * tix:128 * (tix + 1)],
            pbx[:, :].rearrange("p (c n) -> p c n", c=8)), s_dve)

    wt(DVE, tok_xnev2[0])
    t_r1 = rms_to_R(lambda ci: xn_own[:, ci, :], 8, 1.0 / D, sq_a, rr1,
                    dep=tok_xnev2[0])
    t_xns = None
    for dc in range(8):
        wt(DVE, t_r1)
        t_xns = inc(DVE.tensor_tensor(xn_own[:, dc, :], xn_own[:, dc, :],
                                      r_bc[:, :], op=ALU.mult), s_dve)

    wt(SP, tcc)
    t_ro = spdma(o_own[:, :, :],
                 cc_out.ap().rearrange("(ic p) t -> p ic t", p=128)
                 [:, :, bass.ts(rloc, NO)])
    t_woc = None
    for hh in range(2):
        if t_woc is not None:
            wt(GP, t_woc)
        twf = gpdma(wout_scr[:, :, :],
                    wo_d.rearrange("(ic p) d -> p ic d", p=128)[:, 4 * hh:4 * (hh + 1), :])
        wt(DVE, twf)
        t_woc = inc(DVE.tensor_copy(wout_bf[:, 4 * hh:4 * (hh + 1), :],
                                    wout_scr[:, :, :]), s_dve)
    wt(PE, t_woc)
    wt(PE, t_ro)
    SP.wait_ge(s_sp, cnt[s_sp])
    tok_sev = {}
    for tg in range(4):
        tsl = slice(512 * tg, 512 * (tg + 1))
        for dc in range(8):
            i = tg * 8 + dc
            pb = i % 2
            sps = PS[:, 512 * pb:512 * (pb + 1)]
            if (i - 2) in tok_sev:
                wt(PE, tok_sev[i - 2])
            tm = None
            for ic in range(8):
                tm = inc(PE.matmul(sps, wout_bf[:, ic, 128 * dc:128 * (dc + 1)],
                                   o_own[:, ic, tsl], start=(ic == 0), stop=(ic == 7)),
                         s_pe)
            wt(ACT, tm)
            wt(ACT, t_bosb)
            tok_sev[i] = inc(
                ACT.activation(s_bf[:, dc, tsl], sps, AF.Identity,
                               bias=bo_sb[:, dc:dc + 1]), s_act)

    nc.all_engine_barrier()

    if DEBUG:
        gpdma(dbg["s"].rearrange("p (a n) -> p a n", a=8), s_bf[:, :, 0:512])
        GP.wait_ge(s_gp, cnt[s_gp])
        nc.all_engine_barrier()
    if PHASES <= 5:
        return nc, dbg

    tok_upmm = {}
    tok_hev = {}
    for fc in range(32):
        slw = fc % 2
        if fc < 2:
            wt(GP, t_r1)
        if fc >= 2:
            wt(GP, tok_upmm[fc - 2])
        twl = gpdma(wbufu[:, slw, :, :],
                    wuT_d[fc, :, :].rearrange("p (c n) -> p c n", c=8), sem=su[slw])
        for g in range(4):
            i = fc * 4 + g
            pb = i % 2
            sps = PS[:, 512 * pb:512 * (pb + 1)]
            wt(PE, twl)
            wt(PE, t_xns)
            if (i - 2) in tok_hev:
                wt(PE, tok_hev[i - 2])
            tm = None
            for dc in range(8):
                tm = inc(PE.matmul(sps, wbufu[:, slw, dc, :],
                                   xn_own[:, dc, 512 * g:512 * (g + 1)],
                                   start=(dc == 0), stop=(dc == 7)), s_pe)
            tok_upmm[fc] = tm
            wt(ACT, tm)
            tok_hev[i] = inc(
                ACT.activation(h_fm[:, fc, 512 * g:512 * (g + 1)], sps, AF.Gelu),
                s_act)

    t_r2 = rms_to_R(lambda ci: h_fm[:, ci, :], 32, 1.0 / FF, sq_b, rr2,
                    dep=tok_hev[127])

    tok_gev = {}
    tok_dnh = {}
    t_sg_last = None
    wbufd_h = wbufd[:, :, :].rearrange("p (hh c) n -> p hh c n", hh=2)
    for dc in range(8):
        if dc == 0:
            wt(GP, tok_upmm[31])
        if (dc - 1, 0) in tok_dnh:
            wt(GP, tok_dnh[(dc - 1, 0)])
        twl0 = gpdma(wbufd_h[:, 0, :, :],
                     wdT_d[dc, :, 0:2048].rearrange("p (c n) -> p c n", c=16),
                     sem=su[0])
        if (dc - 1, 1) in tok_dnh:
            wt(GP, tok_dnh[(dc - 1, 1)])
        twl1 = gpdma(wbufd_h[:, 1, :, :],
                     wdT_d[dc, :, 2048:4096].rearrange("p (c n) -> p c n", c=16),
                     sem=su[1])
        for g in range(4):
            i = dc * 4 + g
            pb = i % 2
            sps = PS[:, 512 * pb:512 * (pb + 1)]
            wt(PE, twl0)
            wt(PE, t_r2)
            if (i - 2) in tok_gev:
                wt(PE, tok_gev[i - 2])
            tm = None
            for fc in range(16):
                tm = inc(PE.matmul(sps, wbufd_h[:, 0, fc, :],
                                   h_fm[:, fc, 512 * g:512 * (g + 1)],
                                   start=(fc == 0), stop=False), s_pe)
            tok_dnh[(dc, 0)] = tm
            wt(PE, twl1)
            for fc in range(16, 32):
                tm = inc(PE.matmul(sps, wbufd_h[:, 1, fc - 16, :],
                                   h_fm[:, fc, 512 * g:512 * (g + 1)],
                                   start=False, stop=(fc == 31)), s_pe)
            tok_dnh[(dc, 1)] = tm
            wt(DVE, tm)
            tgp = inc(DVE.tensor_tensor(gpre[:, pb, :], sps,
                                        r_bc[:, 512 * g:512 * (g + 1)], op=ALU.mult),
                      s_dve)
            wt(ACT, tgp)
            tsi = inc(ACT.activation(gpre[:, pb, :], gpre[:, pb, :], AF.Silu), s_act)
            wt(DVE, tsi)
            tok_gev[i] = t_sg_last = inc(
                DVE.tensor_tensor(s_bf[:, dc, 512 * g:512 * (g + 1)],
                                  gpre[:, pb, :],
                                  s_bf[:, dc, 512 * g:512 * (g + 1)], op=ALU.mult),
                s_dve)

    nc.all_engine_barrier()

    tok_add = [None, None]
    tok_store = [None, None]
    for tix in range(16):
        sl = tix % 2
        if tok_store[sl] is not None:
            wt(SP, tok_store[sl])
        t_x = spdma(xo_ring[:, sl, :], xo_d[128 * tix:128 * (tix + 1), :], sem=so[sl])
        pbf = bankb(3 if sl == 0 else 5)
        if tok_add[sl] is not None:
            wt(PE, tok_add[sl])
        ttp = None
        for c in range(8):
            ttp = inc(PE.transpose(pbf[:, 128 * c:128 * (c + 1)],
                                   s_bf[:, c, 128 * tix:128 * (tix + 1)],
                                   identB[:, :]), s_pe)
        wt(DVE, ttp)
        wt(DVE, t_x)
        tok_add[sl] = inc(DVE.tensor_tensor(xo_ring[:, sl, :], pbf[:, :],
                                            xo_ring[:, sl, :], op=ALU.add), s_dve)
        wt(SP, tok_add[sl])
        tok_store[sl] = spdma(out_d[128 * tix:128 * (tix + 1), :], xo_ring[:, sl, :],
                              sem=so[sl])

    SP.wait_ge(so[0], cnt[so[0]])
    SP.wait_ge(so[1], cnt[so[1]])
    nc.all_engine_barrier()
    return nc, dbg



def kernel(L0_attn_x, L0_attn_w_qkv, L0_attn_w_out, L0_attn_b_out,
           L0_ffn_w_up, L0_ffn_w_down):
    x = np.ascontiguousarray(np.asarray(L0_attn_x, dtype=np.float32))
    wqkv = np.ascontiguousarray(np.asarray(L0_attn_w_qkv, dtype=np.float32))
    wo = np.ascontiguousarray(np.asarray(L0_attn_w_out, dtype=np.float32))
    bo = np.ascontiguousarray(np.asarray(L0_attn_b_out, dtype=np.float32)).reshape(8, 128)
    wu = np.ascontiguousarray(np.asarray(L0_ffn_w_up, dtype=np.float32))
    wd = np.ascontiguousarray(np.asarray(L0_ffn_w_down, dtype=np.float32))

    nc, _ = build_program()
    in_maps = []
    for c in range(8):
        b = c // 4
        r = c % 4
        h0 = 4 * r
        in_maps.append({
            "x": np.ascontiguousarray(x[b]),
            "xo": np.ascontiguousarray(x[b, NO * r:NO * (r + 1)]),
            "wq": np.ascontiguousarray(wqkv[:, 64 * h0:64 * h0 + 256]),
            "wk": np.ascontiguousarray(wqkv[:, 1024 + 64 * h0:1024 + 64 * h0 + 256]),
            "wv": np.ascontiguousarray(wqkv[:, 2048 + 64 * h0:2048 + 64 * h0 + 256]),
            "wo": wo,
            "bo": bo,
            "wu": wu,
            "wd": wd,
        })
    res = run_bass_kernel_spmd(nc, in_maps, core_ids=list(range(8)))
    outs = [res.results[c]["out"] for c in range(8)]
    full = np.concatenate(outs, axis=0).reshape(2, N, D)
    return full


# version 07
# speedup vs baseline: 1.0311x; 1.0311x over previous
import numpy as np
import concourse.bass as bass
import concourse.mybir as mybir
from concourse.bass_utils import run_bass_kernel_spmd

F32 = mybir.dt.float32
BF16 = mybir.dt.bfloat16
AF = mybir.ActivationFunctionType
ALU = mybir.AluOpType
AX = mybir.AxisListType

PI = float(np.pi)
MAGIC = 12582912.0
INV2PI = float(np.float64(1.0) / (2.0 * np.pi))
CW1 = float(np.float32(6.28125))
CW2 = float(np.float32(2.0 * np.pi - 6.28125))
CW3 = float(2.0 * np.pi - 6.28125 - float(np.float32(2.0 * np.pi - 6.28125)))
LN1E4_32 = float(np.log(10000.0) / 32.0)
LN0125 = float(np.log(0.125))

N = 8192
D = 1024
NT = 64
NO = 2048
FF = 4096
DEBUG = False
PHASES = 6
SIM_REDUCE = False


def build_program():
    nc = bass.Bass("TRN2", target_bir_lowering=False, debug=False)

    x_d = nc.dram_tensor("x", [N, D], F32, kind="ExternalInput").ap()
    xo_d = nc.dram_tensor("xo", [NO, D], F32, kind="ExternalInput").ap()
    wq_d = nc.dram_tensor("wq", [D, 256], F32, kind="ExternalInput").ap()
    wk_d = nc.dram_tensor("wk", [D, 256], F32, kind="ExternalInput").ap()
    wv_d = nc.dram_tensor("wv", [D, 256], F32, kind="ExternalInput").ap()
    wo_d = nc.dram_tensor("wo", [D, D], F32, kind="ExternalInput").ap()
    bo_d = nc.dram_tensor("bo", [8, 128], F32, kind="ExternalInput").ap()
    wu_d = nc.dram_tensor("wu", [FF, D], F32, kind="ExternalInput").ap()
    wd_d = nc.dram_tensor("wd", [D, FF], F32, kind="ExternalInput").ap()
    out_d = nc.dram_tensor("out", [NO, D], F32, kind="ExternalOutput").ap()

    cc_in = nc.dram_tensor("cc_in", [256, N], BF16, kind="Internal")
    cc_out = nc.dram_tensor("cc_out", [1024, N], BF16, kind="Internal")
    wuT_d = nc.dram_tensor("wuT", [32, 128, D], BF16, kind="Internal").ap()
    wdT_d = nc.dram_tensor("wdT", [8, 128, FF], BF16, kind="Internal").ap()

    dbg = {}
    if DEBUG:
        def dob(nm, sh):
            dbg[nm] = nc.dram_tensor("dbg_" + nm, sh, F32, kind="ExternalOutput").ap()
        dob("q", [128, 2 * N])
        dob("k", [128, 2 * N])
        dob("v", [128, 2 * 160])
        dob("o", [128, 2 * N])
        dob("s", [128, 8 * 512])
        dob("h", [128, 2 * NO])
        dob("m", [128, 8])

    base = (nc.sbuf_base + 31) // 32 * 32
    AVAIL = nc.SBUF_PARTITION_SIZE_BYTES - base

    def at(name, shape, dtype, off):
        assert off % 32 == 0, (name, off)
        return nc.alloc_sbuf_tensor_at(name, shape, dtype, offset=base + off).ap()

    U0, W0 = 0, 131072
    P0 = W0 + 45056
    S0 = P0 + 32768
    TOTAL = S0 + 3840
    assert TOTAL <= AVAIL, (TOTAL, AVAIL)

    q_dm = at("q_dm", [128, 2, N], BF16, U0)
    k_dm = at("k_dm", [128, 2, N], BF16, U0 + 32768)
    v_tm = at("v_tm", [128, NT, 2, 160], BF16, U0 + 65536)
    cos_t = at("cos_t", [128, NT, 32], BF16, U0 + 106496)
    sin_t = at("sin_t", [128, NT, 32], BF16, U0 + 110592)
    afin = at("afin", [1, 1040], F32, U0 + 122880)
    wf_ring = at("wf_ring", [128, 2, 1024], F32, P0 + 0)
    tern_scr = at("tern_scr", [128, 2, 2, 1024], BF16, P0 + 8192)
    wspill = at("wspill", [128, 2, 1024], BF16, P0 + 16384)
    absw_scr = at("absw_scr", [128, 2, 1024], BF16, P0 + 20480)
    o_own = at("o_own", [128, 8, NO], BF16, U0 + 0)
    wout_scr = at("wout_scr", [128, 4, 1024], F32, U0 + 32768)
    h_fm = at("h_fm", [128, 32, NO], BF16, U0 + 0)

    xin = at("xin", [128, 2, 1024], F32, W0 + 0)
    xln = at("xln", [128, 2, 1024], BF16, W0 + 8192)
    xnE = at("xnE", [128, 2, 8, 128], BF16, W0 + 12288)
    qkr = at("qkr", [128, 2, 512], BF16, W0 + 16384)
    wqk = at("wqk", [128, 8, 512], BF16, W0 + 18432)
    wv_s = at("wv_s", [128, 8, 256], BF16, W0 + 26624)
    rp_s = at("rp_s", [128, 4, 256], F32, W0 + 30720)
    th64 = at("th64", [1, 2048], F32, W0 + 34816)
    qkc = at("qkc", [128, 2, 512], BF16, W0 + 34816)
    rp_b = at("rp_b", [128, 2, 2, 256], BF16, W0 + 36864)
    tth = at("tth", [1, 2048], F32, W0 + 0)
    prow = at("prow", [1, 128], F32, W0 + 43008)
    pt_ring = at("pt_ring", [128, 3, 1024], BF16, W0 + 0)
    rcps = at("rcps", [128, 512], F32, W0 + 6144)
    qn_scr = at("qn_scr", [128, 512], BF16, W0 + 8192)
    runmax = at("runmax", [128, 4], F32, W0 + 9216)
    mrow = at("mrow", [128, 16], F32, W0 + 9280)
    wout_bf = at("wout_bf", [128, 8, 1024], BF16, U0 + 65536)
    xn_own = at("xn_own", [128, 8, NO], BF16, W0 + 0)
    gpre = at("gpre", [128, 2, 512], F32, W0 + 0)
    rr2 = at("rr2", [128, 512], F32, W0 + 16384)
    wbufd = at("wbufd", [128, 32, 128], BF16, W0 + 4096)
    xo_ring = at("xo_ring", [128, 2, 1024], F32, W0 + 16384)
    sq_b = at("sq_b", [128, 2048], BF16, W0 + 24576)
    r_bc = at("r_bc", [128, 2048], F32, W0 + 32768)
    wbufu = at("wbufu", [128, 2, 8, 128], BF16, W0 + 40960)
    sq_a = at("sq_a", [128, 2048], BF16, W0 + 40960)
    xin_d = at("xin_d", [128, 2, 1024], F32, U0 + 49152)
    xln_d = at("xln_d", [128, 2, 1024], BF16, U0 + 57344)
    bn_d = at("bn_d", [128, 32], F32, U0 + 61440)
    rr1 = at("rr1", [128, 512], F32, U0 + 61568)

    o_all = at("o_all", [128, 2, N], BF16, P0)
    s_bf = at("s_bf", [128, 8, NO], BF16, P0)

    so = [S0]

    def small(name, shape, dtype):
        nbytes = int(np.prod(shape[1:])) * (2 if dtype == BF16 else 4)
        off = so[0]
        so[0] = (off + nbytes + 31) // 32 * 32
        return at(name, shape, dtype, off)

    identB = small("identB", [128, 128], BF16)
    identF = small("identF", [128, 128], F32)
    onesP = small("onesP", [128, 128], F32)
    pA = small("pA", [128, 128], F32)
    pB = small("pB", [128, 128], F32)
    ind0 = small("ind0", [128, 128], BF16)
    ind64 = small("ind64", [128, 128], BF16)
    mhalf = small("mhalf", [128, 128], BF16)
    msum = small("msum", [128, 1], F32)
    ones1 = small("ones1", [128, 1], BF16)
    mbias = small("mbias", [128, 2], F32)
    epscol = small("epscol", [128, 1], F32)
    l0125 = small("l0125", [128, 1], F32)
    thru = small("thru", [128, 1], F32)
    nthru = small("nthru", [128, 1], F32)
    thrd = small("thrd", [128, 1], F32)
    nthrd = small("nthrd", [128, 1], F32)
    bo_sb = small("bo_sb", [128, 8], F32)
    assert so[0] - S0 <= 4096

    PS = nc.alloc_psum_tensor("PS", [128, 4096], F32).ap()

    def bankb(b, n=1):
        return PS[:, 512 * b:512 * (b + n)].bitcast(BF16)

    s_sp = nc.alloc_semaphore("s_sp")
    s_gp = nc.alloc_semaphore("s_gp")
    s_pe = nc.alloc_semaphore("s_pe")
    s_act = nc.alloc_semaphore("s_act")
    s_dve = nc.alloc_semaphore("s_dve")
    s_pool = nc.alloc_semaphore("s_pool")
    s_cc = nc.alloc_semaphore("s_cc")
    sx = [nc.alloc_semaphore("sx0"), nc.alloc_semaphore("sx1")]
    sw = [nc.alloc_semaphore("sw0"), nc.alloc_semaphore("sw1")]
    ssp1 = nc.alloc_semaphore("ssp1")
    ssp2 = nc.alloc_semaphore("ssp2")
    su = [nc.alloc_semaphore("su0"), nc.alloc_semaphore("su1")]
    sdd = nc.alloc_semaphore("sdd")
    so = [nc.alloc_semaphore("so0"), nc.alloc_semaphore("so1")]
    cnt = {s_sp: 0, s_gp: 0, s_pe: 0, s_act: 0, s_dve: 0, s_pool: 0, s_cc: 0,
           ssp1: 0, ssp2: 0, sdd: 0}
    for _s in sx + sw + su + so:
        cnt[_s] = 0

    def inc(inst, sem, d=1):
        inst.then_inc(sem, d)
        cnt[sem] += d
        return (sem, cnt[sem])

    def wt(eng, tok):
        if tok is not None:
            eng.wait_ge(tok[0], tok[1])

    SP, PE, ACT, DVE, GP = nc.sync, nc.tensor, nc.scalar, nc.vector, nc.gpsimd

    def spdma(dst, src, sem=None):
        return inc(SP.dma_start(out=dst, in_=src), sem or s_sp, 16)

    def gpdma(dst, src, sem=None):
        return inc(GP.dma_start(out=dst, in_=src), sem or s_gp, 16)

    GP.iota(prow[:, :], pattern=[[1, 128]], base=0, channel_multiplier=0,
            allow_small_or_imprecise_dtypes=True)
    GP.iota(th64[:, :], pattern=[[0, 64], [1, 32]], base=0, channel_multiplier=0,
            allow_small_or_imprecise_dtypes=True)
    GP.iota(tth[:, :], pattern=[[128, 64], [0, 32]], base=0, channel_multiplier=0,
            allow_small_or_imprecise_dtypes=True)
    def gpop(inst):
        tok = inc(inst, s_pool)
        GP.wait_ge(tok[0], tok[1])
        return tok
    for ap_ in (identB, identF):
        gpop(GP.memset(ap_[:, :], 0.0))
        gpop(GP.affine_select(out=ap_[:, :], in_=ap_[:, :], compare_op=ALU.not_equal,
                              fill=1.0, base=0, pattern=[[-1, 128]], channel_multiplier=1))
    gpop(GP.memset(onesP[:, :], 0.0))
    gpop(GP.memset(onesP[0:1, :], 1.0))
    gpop(GP.memset(onesP[64:65, :], 1.0))
    gpop(GP.memset(pA[:, :], 0.0))
    gpop(GP.memset(pA[64:65, 0:64], 1.0))
    gpop(GP.memset(pB[:, :], 0.0))
    gpop(GP.memset(pB[32:33, 64:128], 1.0))
    gpop(GP.memset(ind0[:, :], 0.0))
    gpop(GP.memset(ind0[:, 0:1], 1.0))
    gpop(GP.memset(ind64[:, :], 0.0))
    gpop(GP.memset(ind64[:, 64:65], 1.0))
    gpop(GP.memset(mhalf[:, :], 0.0))
    gpop(GP.memset(mhalf[0:64, 0:1], 1.0))
    gpop(GP.memset(mhalf[64:128, 64:65], 1.0))
    gpop(GP.memset(msum[:, :], 0.0))
    gpop(GP.memset(msum[0:1, :], 1.0))
    gpop(GP.memset(msum[64:65, :], 1.0))
    gpop(GP.memset(ones1[:, :], 1.0))
    gpop(GP.memset(epscol[:, :], 1e-5))
    gpop(GP.memset(l0125[:, :], LN0125))
    gpop(GP.memset(v_tm[:, :, :, :], 0.0))
    t_cst = gpop(GP.memset(v_tm[:, :, :, 64:65], 1.0))

    wt(ACT, t_cst)
    t_th = inc(ACT.activation(th64[:, :], th64[:, :], AF.Exp, scale=-LN1E4_32), s_act)
    wt(DVE, t_th)
    t_tth = inc(DVE.tensor_mul(tth[:, :], tth[:, :], th64[:, :]), s_dve)
    t_tabs = None
    for b in range(4):
        tb = PS[:, 2560:3072]
        red = rp_s[:, 0:2, :].rearrange("p a d -> p (a d)")
        scr = rp_s[:, 2:4, :].rearrange("p a d -> p (a d)")
        wt(PE, t_tth)
        if b > 0:
            wt(PE, (s_act, cnt[s_act]))
        PE.matmul(tb, prow[:, :], th64[:, 0:512], start=True, stop=False)
        t_ang = inc(PE.matmul(tb, onesP[0:1, :], tth[:, 512 * b:512 * (b + 1)],
                              start=False, stop=True), s_pe)
        wt(DVE, t_ang)
        t = inc(DVE.tensor_scalar(scr, tb, INV2PI, MAGIC, op0=ALU.mult, op1=ALU.add), s_dve)
        wt(DVE, t)
        t = inc(DVE.tensor_scalar_add(scr, scr, -MAGIC), s_dve)
        wt(DVE, t)
        t = inc(DVE.scalar_tensor_tensor(red, scr, -CW1, tb, op0=ALU.mult, op1=ALU.add), s_dve)
        wt(DVE, t)
        t = inc(DVE.scalar_tensor_tensor(red, scr, -CW2, red, op0=ALU.mult, op1=ALU.add), s_dve)
        wt(DVE, t)
        t = inc(DVE.scalar_tensor_tensor(red, scr, -CW3, red, op0=ALU.mult, op1=ALU.add), s_dve)
        wt(DVE, t)
        t = inc(DVE.tensor_scalar_min(red, red, PI), s_dve)
        wt(DVE, t)
        tred = inc(DVE.tensor_scalar_max(red, red, -PI), s_dve)
        wt(ACT, tred)
        tsin = inc(ACT.activation(
            sin_t[:, 16 * b:16 * (b + 1), :].rearrange("p a d -> p (a d)"),
            red, AF.Sin), s_act)
        wt(DVE, tred)
        t = inc(DVE.tensor_scalar_add(scr, red, PI / 2), s_dve)
        wt(DVE, t)
        wt(DVE, tsin)
        t = inc(DVE.tensor_single_scalar(red, scr, PI, op=ALU.is_gt), s_dve)
        wt(DVE, t)
        t = inc(DVE.scalar_tensor_tensor(scr, red, -2 * PI, scr, op0=ALU.mult, op1=ALU.add), s_dve)
        wt(ACT, t)
        wt(ACT, tsin)
        t_tabs = inc(ACT.activation(
            cos_t[:, 16 * b:16 * (b + 1), :].rearrange("p a d -> p (a d)"),
            scr, AF.Sin), s_act)

    def load_w256(dst_slice, src):
        toks = []
        for hh in range(2):
            tl = spdma(xin[:, hh, :].rearrange("p (c n) -> p c n", c=4),
                       src.rearrange("(c p) n -> p c n", p=128)[:, 4 * hh:4 * (hh + 1), :],
                       sem=sx[hh])
            toks.append(tl)
        wt(DVE, toks[0])
        wt(DVE, toks[1])
        t0 = inc(DVE.tensor_copy(dst_slice(0), xin[:, 0, :].rearrange("p (c n) -> p c n", c=4)), s_dve)
        t1 = inc(DVE.tensor_copy(dst_slice(1), xin[:, 1, :].rearrange("p (c n) -> p c n", c=4)), s_dve)
        wt(SP, t1)
        return t1

    wt(SP, (s_pe, cnt[s_pe]))
    wt(SP, t_tabs)
    load_w256(lambda hh: wqk[:, 4 * hh:4 * (hh + 1), 0:256], wq_d)
    t_wq = load_w256(lambda hh: wqk[:, 4 * hh:4 * (hh + 1), 256:512], wk_d)
    t_wv = load_w256(lambda hh: wv_s[:, 4 * hh:4 * (hh + 1), :], wv_d)

    wt(GP, (s_dve, cnt[s_dve]))
    t_bo = gpdma(rp_s[0:8, 0, 0:128], bo_d[:, :])
    wt(PE, t_bo)
    wt(PE, t_tabs)
    t = inc(PE.transpose(PS[0:128, 2560:2568], rp_s[0:8, 0, 0:128], identF[0:8, 0:8]), s_pe)
    wt(DVE, t)
    t_bosb = inc(DVE.tensor_copy(bo_sb[:, :], PS[:, 2560:2568]), s_dve)

    def blocks_up():
        return [(wu_d[128 * i:128 * (i + 1), :], wuT_d[i, :, :]) for i in range(32)]

    def blocks_dn():
        out = []
        for dcq in range(32):
            dc, q4 = dcq // 4, dcq % 4
            out.append((wd_d[128 * dc:128 * (dc + 1), 1024 * q4:1024 * (q4 + 1)],
                        wdT_d[dc, :, 1024 * q4:1024 * (q4 + 1)]))
        return out

    AB7 = PS[0:1, 3584:4096]

    def alpha_jobs(blocks, state, dep):
        nb = len(blocks)
        for i, (src, _) in enumerate(blocks):
            def load(i=i, src=src):
                sl = i % 2
                if state.get("abs%d" % sl) is not None:
                    wt(GP, state["abs%d" % sl])
                elif state.get("war") is not None:
                    wt(GP, state["war"])
                state["ld%d" % sl] = gpdma(wf_ring[:, sl, :], src, sem=sw[sl])

            def comp(i=i):
                sl = i % 2
                if state.get("mm%d" % sl) is not None:
                    wt(ACT, state["mm%d" % sl])
                wt(ACT, state["ld%d" % sl])
                wt(ACT, dep)
                if i == 0 and state.get("war") is not None:
                    wt(ACT, state["war"])
                state["abs%d" % sl] = state["abs"] = inc(
                    ACT.activation(absw_scr[:, sl, :], wf_ring[:, sl, :], AF.Abs),
                    s_act)
                wt(PE, state["abs"])
                if i == 0 and state.get("war") is not None:
                    wt(PE, state["war"])
                PE.matmul(AB7, ones1[:, :], absw_scr[:, sl, 512:1024],
                          start=(i == 0), stop=False)
                state["mm%d" % sl] = state["mm"] = inc(
                    PE.matmul(AB7, ones1[:, :], absw_scr[:, sl, 0:512],
                              start=False, stop=(i == nb - 1)), s_pe)
            yield (load, comp)

    def alpha_fin(state, nb, thr_ap, nthr_ap):
        tok_mm = state["mm"]
        wt(DVE, tok_mm)
        t = inc(DVE.tensor_copy(afin[0:1, 0:512], AB7[0:1, 0:512]), s_dve)
        wt(DVE, t)
        t = inc(DVE.reduce_sum(afin[0:1, 1024:1025], afin[0:1, 0:512], axis=AX.X),
                s_dve)
        wt(DVE, t)
        t = inc(DVE.tensor_scalar_mul(afin[0:1, 1024:1025], afin[0:1, 1024:1025],
                                      0.5 / (nb * 128 * 1024)), s_dve)
        wt(PE, t)
        wt(PE, tok_mm)
        t = inc(PE.matmul(PS[0:128, 3584:3585], onesP[0:1, :], afin[0:1, 1024:1025],
                          start=True, stop=True), s_pe)
        wt(DVE, t)
        t = inc(DVE.tensor_copy(thr_ap[:, :], PS[0:128, 3584:3585]), s_dve)
        wt(DVE, t)
        t = inc(DVE.tensor_scalar_mul(nthr_ap[:, :], thr_ap[:, :], -1.0), s_dve)
        state["thr"] = t
        return t

    def spill_jobs(blocks, state):
        for i, (src, dst, thr_ap, nthr_ap, dep_key) in enumerate(blocks):
            def load(i=i, src=src, dep_key=dep_key):
                sl = i % 2
                if state.get("tern%d" % sl) is not None:
                    wt(GP, state["tern%d" % sl])
                if i < 2:
                    wt(GP, state[dep_key])
                state["ld%d" % sl] = gpdma(wf_ring[:, sl, :], src, sem=sw[sl])

            def comp(i=i, dst=dst, thr_ap=thr_ap, nthr_ap=nthr_ap,
                     dep_key=dep_key):
                sl = i % 2
                wt(DVE, state["ld%d" % sl])
                wt(DVE, state[dep_key])
                if state.get("T%d" % sl) is not None:
                    wt(DVE, state["T%d" % sl])
                t = inc(DVE.tensor_scalar(tern_scr[:, sl, 0, :], wf_ring[:, sl, :],
                                          thr_ap[:, :], None, op0=ALU.is_gt),
                        s_dve)
                wt(DVE, t)
                t = inc(DVE.tensor_scalar(tern_scr[:, sl, 1, :], wf_ring[:, sl, :],
                                          nthr_ap[:, :], None, op0=ALU.is_lt),
                        s_dve)
                wt(DVE, t)
                t_wt = inc(DVE.tensor_sub(tern_scr[:, sl, 0, :],
                                          tern_scr[:, sl, 0, :],
                                          tern_scr[:, sl, 1, :]), s_dve)
                state["tern%d" % sl] = state["tern"] = t_wt
                pb = bankb(6)
                wt(PE, t_wt)
                if state.get("ev") is not None:
                    wt(PE, state["ev"])
                ttp = None
                for ci in range(8):
                    ttp = inc(PE.transpose(pb[:, 128 * ci:128 * (ci + 1)],
                                           tern_scr[:, sl, 0, 128 * ci:128 * (ci + 1)],
                                           identB[:, :]), s_pe)
                state["T%d" % sl] = ttp
                wt(DVE, ttp)
                if state.get("sp%d" % sl) is not None:
                    wt(DVE, state["sp%d" % sl])
                tok_ev = inc(DVE.tensor_copy(wspill[:, sl, :], pb[:, :]), s_dve)
                state["ev"] = tok_ev
                wt(GP, tok_ev)
                state["sp%d" % sl] = gpdma(dst, wspill[:, sl, :],
                                           sem=(ssp1, ssp2)[sl])
            yield (load, comp)

    st_au = {}
    st_ad = {}

    st_sp = {}

    def fin_up_job():
        alpha_fin(st_au, 32, thru, nthru)
        st_sp["thr_up"] = st_au["thr"]
        st_ad["war"] = st_au["thr"]

    def fin_dn_job():
        alpha_fin(st_ad, 32, thrd, nthrd)
        st_sp["thr_dn"] = st_ad["thr"]

    spill_blocks = ([(s, d, thru, nthru, "thr_dn") for (s, d) in blocks_up()] +
                    [(s, d, thrd, nthrd, "thr_dn") for (s, d) in blocks_dn()])
    wjobs = (list(alpha_jobs(blocks_up(), st_au, t_bosb)) + [(None, fin_up_job)] +
             list(alpha_jobs(blocks_dn(), st_ad, t_bosb)) + [(None, fin_dn_job)] +
             list(spill_jobs(spill_blocks, st_sp)))
    wjob_i = [0]
    wload_i = [0]

    def run_wjobs(target):
        tgt = min(target, len(wjobs))
        while wjob_i[0] < tgt:
            while wload_i[0] < min(wjob_i[0] + 2, len(wjobs)):
                ld = wjobs[wload_i[0]][0]
                if ld is None:
                    if wload_i[0] >= wjob_i[0]:
                        break
                    wload_i[0] += 1
                    continue
                ld()
                wload_i[0] += 1
            wjobs[wjob_i[0]][1]()
            wjob_i[0] += 1

    SP.wait_ge(sx[0], cnt[sx[0]])
    SP.wait_ge(sx[1], cnt[sx[1]])
    nc.all_engine_barrier()
    if PHASES <= 1:
        return nc, dbg

    def emit_ln_tile(src_rows, sl, dep_war_xin, dep_war_xln, xin_ap, xln_ap, bnp,
                     t_x):
        bn_ap = bnp[:, 16 * sl:16 * (sl + 1)]
        wt(DVE, t_x)
        t1 = inc(DVE.bn_stats(bn_ap[:, 0:6], xin_ap[:, sl, 0:512]), s_dve)
        t2 = inc(DVE.bn_stats(bn_ap[:, 6:12], xin_ap[:, sl, 512:1024]), s_dve)
        wt(DVE, t2)
        t3 = inc(DVE.bn_aggr(bn_ap[:, 12:14], bn_ap[:, 0:12]), s_dve)
        wt(ACT, t3)
        t4 = inc(ACT.activation(bn_ap[:, 14:15], bn_ap[:, 13:14], AF.Ln,
                                bias=epscol[:, :]), s_act)
        wt(ACT, t4)
        t5 = inc(ACT.activation(bn_ap[:, 14:15], bn_ap[:, 14:15], AF.Exp,
                                scale=-0.5), s_act)
        wt(DVE, t5)
        t5b = inc(DVE.scalar_tensor_tensor(bn_ap[:, 15:16], bn_ap[:, 12:13], -1.0,
                                           bn_ap[:, 14:15], op0=ALU.mult,
                                           op1=ALU.mult), s_dve)
        wt(ACT, t5b)
        if dep_war_xln is not None:
            wt(ACT, dep_war_xln)
        t6 = inc(ACT.activation(xln_ap[:, sl, :], xin_ap[:, sl, :], AF.Identity,
                                bias=bn_ap[:, 15:16], scale=bn_ap[:, 14:15]),
                 s_act)
        return t6

    def ln_dma(src_rows, sl, dep_war_xin, xin_ap):
        if dep_war_xin is not None:
            wt(SP, dep_war_xin)
        return spdma(xin_ap[:, sl, :], src_rows, sem=sx[sl])

    bn_a = rp_s[:, 0, 0:32]

    a_xin_war = [None, None]
    a_xln_war = [None, None]
    tok_xnev_last = [None]
    a_xnE_war = [None, None]
    tok_rope = [None, None]
    a_qkr_war = [None, None]
    a_qkc_war = [None, None]
    tok_vev_prev = None
    tok_qkev_prev = None

    a_xdma = [None, None]
    a_xdma[0] = ln_dma(x_d[0:128, :], 0, None, xin)
    VBANK = [PS[:, 1024:1280], PS[:, 2560:2816]]
    a_vev = [None, None]
    a_tb_war = [None, None]
    for tix in range(NT):
        if tix + 1 < NT:
            slp = (tix + 1) % 2
            a_xdma[slp] = ln_dma(x_d[128 * (tix + 1):128 * (tix + 2), :], slp,
                                 a_xin_war[slp], xin)
        run_wjobs((tix + 1) * 131 // 64)
        sl = tix % 2
        t_xln = emit_ln_tile(x_d[128 * tix:128 * (tix + 1), :], sl,
                             a_xin_war[sl], a_xln_war[sl], xin, xln, bn_a,
                             a_xdma[sl])
        a_xin_war[sl] = t_xln
        pbx = bankb(3 if sl == 0 else 4)
        wt(PE, t_xln)
        if a_tb_war[sl] is not None:
            wt(PE, a_tb_war[sl])
        ttp = None
        for c in range(8):
            ttp = inc(PE.transpose(pbx[:, 128 * c:128 * (c + 1)],
                                   xln[:, sl, 128 * c:128 * (c + 1)],
                                   identB[:, :]), s_pe)
        a_xln_war[sl] = ttp
        wt(DVE, ttp)
        if a_xnE_war[sl] is not None:
            wt(DVE, a_xnE_war[sl])
        t_xnE = inc(DVE.tensor_copy(xnE[:, sl, :, :],
                                    pbx[:, :].rearrange("p (c n) -> p c n", c=8)),
                    s_dve)
        wt(PE, t_xnE)
        wt(PE, t_wv)
        if a_vev[sl] is not None:
            wt(PE, a_vev[sl])
        tvm = None
        for c in range(8):
            tvm = inc(PE.matmul(VBANK[sl], xnE[:, sl, c, :], wv_s[:, c, :],
                                start=(c == 0), stop=(c == 7)), s_pe)
        qb = sl
        wt(PE, t_wq)
        if tok_rope[qb] is not None:
            wt(PE, tok_rope[qb])
        tqk = None
        for c in range(8):
            tqk = inc(PE.matmul(PS[:, 512 * qb:512 * (qb + 1)], xnE[:, sl, c, :],
                                wqk[:, c, :], start=(c == 0), stop=(c == 7)), s_pe)
        a_xnE_war[sl] = tqk
        wt(ACT, tvm)
        vdst = bass.AP(v_tm.tensor, tix * 320,
                       [[20480, 128], [160, 2], [96, 2], [1, 64]])
        a_vev[sl] = inc(ACT.activation(
            vdst, VBANK[sl].rearrange("p (a h d) -> p a h d", a=2, h=2),
            AF.Identity), s_act)
        qkp = PS[:, 512 * qb:512 * (qb + 1)]
        wt(ACT, tqk)
        wt(ACT, t_tabs)
        if a_qkc_war[sl] is not None:
            wt(ACT, a_qkc_war[sl])
        t_qkc = inc(ACT.activation(qkc[:, sl, :], qkp, AF.Identity), s_act)
        qv = qkc[:, sl, :].rearrange("p (h d) -> p h d", d=64)
        x1 = qv[:, :, 0:32]
        x2 = qv[:, :, 32:64]
        cosv = cos_t[:, tix, :].unsqueeze(1).broadcast_to([128, 8, 32])
        sinv = sin_t[:, tix, :].unsqueeze(1).broadcast_to([128, 8, 32])
        r3 = rp_b[:, sl, 0, :].rearrange("p (a d) -> p a d", d=32)[:, 0:8, :]
        r4 = rp_b[:, sl, 1, :].rearrange("p (a d) -> p a d", d=32)[:, 0:8, :]
        o1 = qkr[:, sl, :].rearrange("p (h d) -> p h d", d=64)[:, :, 0:32]
        o2 = qkr[:, sl, :].rearrange("p (h d) -> p h d", d=64)[:, :, 32:64]
        wt(DVE, t_qkc)
        wt(DVE, t_tabs)
        if a_qkr_war[sl] is not None:
            wt(DVE, a_qkr_war[sl])
        DVE.tensor_tensor(r3, x1, cosv, op=ALU.mult)
        tr = inc(DVE.tensor_tensor(r4, x2, sinv, op=ALU.mult), s_dve)
        wt(DVE, tr)
        tr = inc(DVE.tensor_sub(o1, r3, r4), s_dve)
        wt(DVE, tr)
        DVE.tensor_tensor(r3, x1, sinv, op=ALU.mult)
        tr = inc(DVE.tensor_tensor(r4, x2, cosv, op=ALU.mult), s_dve)
        wt(DVE, tr)
        t_rope = inc(DVE.tensor_add(o2, r3, r4), s_dve)
        tok_rope[qb] = t_rope
        a_qkc_war[sl] = t_rope
        pbq = pbx
        wt(PE, t_rope)
        wt(PE, t_xnE)
        ttq = None
        for c in range(4):
            ttq = inc(PE.transpose(pbq[:, 128 * c:128 * (c + 1)],
                                   qkr[:, sl, 128 * c:128 * (c + 1)],
                                   identB[:, :]), s_pe)
        a_qkr_war[sl] = ttq
        wt(DVE, ttq)
        DVE.tensor_copy(q_dm[:, :, 128 * tix:128 * (tix + 1)],
                        pbq[:, 0:256].rearrange("p (a n) -> p a n", a=2))
        a_tb_war[sl] = inc(
            DVE.tensor_copy(k_dm[:, :, 128 * tix:128 * (tix + 1)],
                            pbq[:, 256:512].rearrange("p (a n) -> p a n", a=2)),
            s_dve)

    run_wjobs(len(wjobs))
    GP.wait_ge(ssp1, cnt[ssp1])
    GP.wait_ge(ssp2, cnt[ssp2])
    GP.wait_ge(sw[0], cnt[sw[0]])
    GP.wait_ge(sw[1], cnt[sw[1]])
    GP.wait_ge(s_gp, cnt[s_gp])
    nc.all_engine_barrier()

    if DEBUG:
        gpdma(dbg["q"].rearrange("p (a n) -> p a n", a=2), q_dm[:, :, :])
        gpdma(dbg["k"].rearrange("p (a n) -> p a n", a=2), k_dm[:, :, :])
        gpdma(dbg["v"].rearrange("p (a n) -> p a n", a=2), v_tm[:, 0, :, :])
        GP.wait_ge(s_gp, cnt[s_gp])
        nc.all_engine_barrier()
    if PHASES <= 2:
        return nc, dbg

    tok_run = inc(DVE.memset(runmax[:, :], 0.0), s_dve)
    tok_rmax = None
    for j, src in ((0, q_dm), (1, k_dm)):
        for pr in range(2):
            for g in range(16):
                sl512 = src[:, pr, 512 * g:512 * (g + 1)]
                wt(DVE, tok_rmax)
                tsq = inc(DVE.tensor_mul(qn_scr[:, :], sl512, sl512), s_dve)
                wt(PE, tsq)
                if tok_rmax is not None:
                    wt(PE, tok_rmax)
                tmm = inc(PE.matmul(PS[:, 1536:2048], mhalf[:, :], qn_scr[:, :],
                                    start=True, stop=True), s_pe)
                wt(DVE, tmm)
                tr1 = inc(DVE.reduce_max(rp_s[:, 3, 0:1], PS[:, 1536:2048], axis=AX.X),
                          s_dve)
                wt(DVE, tr1)
                wt(DVE, tok_run)
                ji = 2 * j + pr
                tok_run = inc(DVE.tensor_max(runmax[:, ji:ji + 1],
                                             runmax[:, ji:ji + 1],
                                             rp_s[:, 3, 0:1]), s_dve)
                tok_rmax = tok_run
    wt(DVE, tok_run)
    t = inc(DVE.tensor_mul(runmax[:, 0:2], runmax[:, 0:2], runmax[:, 2:4]), s_dve)
    wt(PE, t)
    t = inc(PE.matmul(PS[0:1, 1536:1538], msum[:, :], runmax[:, 0:2],
                      start=True, stop=True), s_pe)
    wt(DVE, t)
    t = inc(DVE.tensor_copy(mrow[0:1, 0:2], PS[0:1, 1536:1538]), s_dve)
    wt(ACT, t)
    t = inc(ACT.activation(mrow[0:1, 2:4], mrow[0:1, 0:2], AF.Ln), s_act)
    wt(ACT, t)
    t = inc(ACT.activation(mrow[0:1, 4:6], mrow[0:1, 2:4], AF.Exp, scale=0.5,
                           bias=l0125[0:1, :]), s_act)
    wt(DVE, t)
    t = inc(DVE.tensor_scalar_mul(mrow[0:1, 6:8], mrow[0:1, 4:6], -1.0), s_dve)
    wt(PE, t)
    t = inc(PE.matmul(PS[:, 1536:1538], onesP[0:1, :], mrow[0:1, 6:8],
                      start=True, stop=True), s_pe)
    wt(DVE, t)
    t_mb = inc(DVE.tensor_copy(mbias[:, :], PS[:, 1536:1538]), s_dve)

    nc.all_engine_barrier()
    if PHASES <= 3:
        return nc, dbg

    if SIM_REDUCE:
        GP.memset(o_all[:, :, :], 0.0)
        nc.all_engine_barrier()
    NQ = 16
    NKT = 64
    b_pairs = [0] if PHASES in (3.5, 3.7) else range(2)
    b_qgs = [0] if PHASES == 3.5 else range(NQ)
    if SIM_REDUCE:
        b_pairs = range(2)
        b_qgs = [0, 1]
    tok_exp = {}
    tok_pv = {}
    tok_sc = {}
    stage_extra = {}
    prev_norm_tok = None

    def emit_pv(m_, pr_, first, last):
        kt_ = m_ % NKT
        stp = m_ % 3
        wt(PE, tok_exp[m_])
        if first and prev_norm_tok is not None:
            wt(PE, prev_norm_tok)
        PE.matmul(PS[0:65, 3072:3584], v_tm[:, kt_, pr_, 0:65],
                  pt_ring[:, stp, 0:512], start=first, stop=last)
        tok_pv[m_] = inc(
            PE.matmul(PS[:, 3584:4096], v_tm[:, kt_, pr_, 32:160],
                      pt_ring[:, stp, 512:1024], start=first, stop=last), s_pe)

    for pr in b_pairs:
        for qg in b_qgs:
            qslice = slice(512 * qg, 512 * (qg + 1))
            for kt in range(NKT):
                m = (pr * NQ + qg) * NKT + kt
                st = m % 3
                sb_ = PS[:, 1024 * st:1024 * (st + 1)]
                if (m - 3) in tok_exp:
                    wt(PE, tok_exp[m - 3])
                if st in stage_extra:
                    wt(PE, stage_extra.pop(st))
                PE.matmul(sb_[:, 0:512], k_dm[0:64, pr, 128 * kt:128 * (kt + 1)],
                          q_dm[0:64, pr, qslice], start=True, stop=True)
                tok_sc[m] = inc(
                    PE.matmul(sb_[:, 512:1024],
                              k_dm[64:128, pr, 128 * kt:128 * (kt + 1)],
                              q_dm[64:128, pr, qslice], start=True, stop=True), s_pe)
                wt(ACT, tok_sc[m])
                tok_exp[m] = inc(
                    ACT.activation(pt_ring[:, st, :], sb_[:, :], AF.Exp,
                                   bias=mbias[:, pr:pr + 1], scale=0.125), s_act)
                if kt > 1:
                    emit_pv(m - 2, pr, kt - 2 == 0, False)
            m_last = (pr * NQ + qg) * NKT + NKT - 1
            emit_pv(m_last - 1, pr, False, False)
            emit_pv(m_last, pr, False, True)
            wt(ACT, tok_pv[m_last])
            ta = inc(ACT.activation(rcps[64:65, :], PS[64:65, 3072:3584], AF.Ln), s_act)
            wt(ACT, ta)
            ta = inc(ACT.activation(rcps[64:65, :], rcps[64:65, :], AF.Exp,
                                    scale=-1.0), s_act)
            tb_ = inc(ACT.activation(rcps[32:33, :], PS[32:33, 3584:4096], AF.Ln), s_act)
            wt(ACT, tb_)
            tb_ = inc(ACT.activation(rcps[32:33, :], rcps[32:33, :], AF.Exp,
                                     scale=-1.0), s_act)
            rst = (m_last + 2) % 3
            rbankA = PS[:, 1024 * rst:1024 * rst + 512]
            rbankB = PS[:, 1024 * rst + 512:1024 * rst + 1024]
            wt(PE, ta)
            wt(PE, tb_)
            PE.matmul(rbankA, pA[64:65, :], rcps[64:65, :], start=True, stop=True)
            tR = inc(PE.matmul(rbankB, pB[32:33, :], rcps[32:33, :],
                               start=True, stop=True), s_pe)
            wt(DVE, tR)
            DVE.tensor_copy(rcps[0:64, :], rbankA[0:64, :])
            tc = inc(DVE.tensor_copy(rcps[64:128, :], rbankB[64:128, :]), s_dve)
            wt(DVE, tc)
            DVE.tensor_tensor(o_all[0:64, pr, qslice], PS[0:64, 3072:3584],
                              rcps[0:64, :], op=ALU.mult)
            prev_norm_tok = inc(
                DVE.tensor_tensor(o_all[64:128, pr, qslice], PS[64:128, 3584:4096],
                                  rcps[64:128, :], op=ALU.mult), s_dve)
            stage_extra[rst] = tc

    nc.all_engine_barrier()

    if DEBUG:
        gpdma(dbg["o"].rearrange("p (a n) -> p a n", a=2), o_all[:, :, :])
        GP.wait_ge(s_gp, cnt[s_gp])
        nc.all_engine_barrier()
    if PHASES <= 4.5:
        return nc, dbg

    def rms_to_R(get_chunk, nchunk, inv_d, sq_ap, rr_ap, dep=None):
        tok_mm = None
        tsq_prev = dep
        first = {4: True, 5: True}
        lastmm = {}
        for ci in range(nchunk):
            if tsq_prev is not None:
                wt(DVE, tsq_prev)
            tsq = inc(DVE.tensor_mul(sq_ap[:, :], get_chunk(ci), get_chunk(ci)), s_dve)
            wt(PE, tsq)
            for g in range(4):
                bk = PS[:, 512 * (4 + g // 2):512 * (5 + g // 2)]
                ind = ind0 if g % 2 == 0 else ind64
                b_ = 4 + g // 2
                tok_mm = inc(PE.matmul(bk, ind[:, :], sq_ap[:, 512 * g:512 * (g + 1)],
                                       start=first[b_],
                                       stop=(ci == nchunk - 1 and g % 2 == 1)), s_pe)
                first[b_] = False
            tsq_prev = tok_mm
        tlast = None
        for g in range(4):
            bk = PS[:, 512 * (4 + g // 2):512 * (5 + g // 2)]
            rr = 64 * (g % 2)
            wt(ACT, tok_mm)
            if tlast is not None:
                pass
            ta = inc(ACT.activation(rr_ap[rr:rr + 1, :], bk[rr:rr + 1, :], AF.Ln,
                                    scale=inv_d), s_act)
            wt(ACT, ta)
            ta = inc(ACT.activation(rr_ap[rr:rr + 1, :], rr_ap[rr:rr + 1, :], AF.Exp,
                                    scale=0.5), s_act)
            wt(DVE, ta)
            td = inc(DVE.tensor_scalar_add(rr_ap[rr:rr + 1, :], rr_ap[rr:rr + 1, :],
                                           1e-8), s_dve)
            wt(ACT, td)
            ta = inc(ACT.activation(rr_ap[rr:rr + 1, :], rr_ap[rr:rr + 1, :], AF.Ln), s_act)
            wt(ACT, ta)
            ta = inc(ACT.activation(rr_ap[rr:rr + 1, :], rr_ap[rr:rr + 1, :], AF.Exp,
                                    scale=-1.0), s_act)
            wt(PE, ta)
            if tlast is not None:
                wt(PE, tlast)
            tp = inc(PE.matmul(PS[:, 3072:3584], onesP[rr:rr + 1, :],
                               rr_ap[rr:rr + 1, :], start=True, stop=True), s_pe)
            wt(DVE, tp)
            tlast = inc(DVE.tensor_copy(r_bc[:, 512 * g:512 * (g + 1)],
                                        PS[:, 3072:3584]), s_dve)
        return tlast


    rloc = nc.partition_id() % 4
    tgi = gpdma(cc_in.ap().rearrange("(a p) t -> p a t", p=128), o_all[:, :, :])
    wt(GP, tgi)
    if SIM_REDUCE:
        for r4 in range(4):
            tcc = gpdma(cc_out.ap()[256 * r4:256 * (r4 + 1), :], cc_in.ap())
        GP.wait_ge(s_gp, cnt[s_gp])
    else:
        tcc = inc(GP.collective_compute(
            "AllGather", ALU.bypass,
            replica_groups=[[0, 1, 2, 3], [4, 5, 6, 7]],
            ins=[cc_in.ap().opt()], outs=[cc_out.ap().opt()]), s_cc)
    d_xin_war = [None, None]
    d_xln_war = [None, None]
    tok_xnev2 = [None]
    d_xdma = [None, None]
    d_xdma[0] = ln_dma(xo_d[0:128, :], 0, None, xin_d)
    for tix in range(16):
        if tix + 1 < 16:
            slp = (tix + 1) % 2
            d_xdma[slp] = ln_dma(xo_d[128 * (tix + 1):128 * (tix + 2), :], slp,
                                 d_xin_war[slp], xin_d)
        sl = tix % 2
        t_xln = emit_ln_tile(xo_d[128 * tix:128 * (tix + 1), :], sl,
                             d_xin_war[sl], d_xln_war[sl], xin_d, xln_d,
                             bn_d[:, 0:32].rearrange("p n -> p n"), d_xdma[sl])
        d_xin_war[sl] = t_xln
        pbx = bankb(3)
        wt(PE, t_xln)
        if tok_xnev2[0] is not None:
            wt(PE, tok_xnev2[0])
        ttp = None
        for c in range(8):
            ttp = inc(PE.transpose(pbx[:, 128 * c:128 * (c + 1)],
                                   xln_d[:, sl, 128 * c:128 * (c + 1)],
                                   identB[:, :]), s_pe)
        d_xln_war[sl] = ttp
        wt(DVE, ttp)
        tok_xnev2[0] = inc(DVE.tensor_copy(
            xn_own[:, :, 128 * tix:128 * (tix + 1)],
            pbx[:, :].rearrange("p (c n) -> p c n", c=8)), s_dve)

    wt(DVE, tok_xnev2[0])
    t_r1 = rms_to_R(lambda ci: xn_own[:, ci, :], 8, 1.0 / D, sq_a, rr1,
                    dep=tok_xnev2[0])
    t_xns = None
    for dc in range(8):
        wt(DVE, t_r1)
        t_xns = inc(DVE.tensor_tensor(xn_own[:, dc, :], xn_own[:, dc, :],
                                      r_bc[:, :], op=ALU.mult), s_dve)

    wt(SP, tcc)
    t_ro = spdma(o_own[:, :, :],
                 cc_out.ap().rearrange("(ic p) t -> p ic t", p=128)
                 [:, :, bass.ts(rloc, NO)])
    t_woc = None
    for hh in range(2):
        if t_woc is not None:
            wt(GP, t_woc)
        twf = gpdma(wout_scr[:, :, :],
                    wo_d.rearrange("(ic p) d -> p ic d", p=128)[:, 4 * hh:4 * (hh + 1), :])
        wt(DVE, twf)
        t_woc = inc(DVE.tensor_copy(wout_bf[:, 4 * hh:4 * (hh + 1), :],
                                    wout_scr[:, :, :]), s_dve)
    wt(PE, t_woc)
    wt(PE, t_ro)
    SP.wait_ge(s_sp, cnt[s_sp])
    tok_sev = {}
    for tg in range(4):
        tsl = slice(512 * tg, 512 * (tg + 1))
        for dc in range(8):
            i = tg * 8 + dc
            pb = i % 2
            sps = PS[:, 512 * pb:512 * (pb + 1)]
            if (i - 2) in tok_sev:
                wt(PE, tok_sev[i - 2])
            tm = None
            for ic in range(8):
                tm = inc(PE.matmul(sps, wout_bf[:, ic, 128 * dc:128 * (dc + 1)],
                                   o_own[:, ic, tsl], start=(ic == 0), stop=(ic == 7)),
                         s_pe)
            wt(ACT, tm)
            wt(ACT, t_bosb)
            tok_sev[i] = inc(
                ACT.activation(s_bf[:, dc, tsl], sps, AF.Identity,
                               bias=bo_sb[:, dc:dc + 1]), s_act)

    nc.all_engine_barrier()

    if DEBUG:
        gpdma(dbg["s"].rearrange("p (a n) -> p a n", a=8), s_bf[:, :, 0:512])
        GP.wait_ge(s_gp, cnt[s_gp])
        nc.all_engine_barrier()
    if PHASES <= 5:
        return nc, dbg

    tok_upmm = {}
    tok_hev = {}
    for fc in range(32):
        slw = fc % 2
        if fc < 2:
            wt(GP, t_r1)
        if fc >= 2:
            wt(GP, tok_upmm[fc - 2])
        twl = gpdma(wbufu[:, slw, :, :],
                    wuT_d[fc, :, :].rearrange("p (c n) -> p c n", c=8), sem=su[slw])
        for g in range(4):
            i = fc * 4 + g
            pb = i % 2
            sps = PS[:, 512 * pb:512 * (pb + 1)]
            wt(PE, twl)
            wt(PE, t_xns)
            if (i - 2) in tok_hev:
                wt(PE, tok_hev[i - 2])
            tm = None
            for dc in range(8):
                tm = inc(PE.matmul(sps, wbufu[:, slw, dc, :],
                                   xn_own[:, dc, 512 * g:512 * (g + 1)],
                                   start=(dc == 0), stop=(dc == 7)), s_pe)
            tok_upmm[fc] = tm
            wt(ACT, tm)
            tok_hev[i] = inc(
                ACT.activation(h_fm[:, fc, 512 * g:512 * (g + 1)], sps, AF.Gelu),
                s_act)

    t_r2 = rms_to_R(lambda ci: h_fm[:, ci, :], 32, 1.0 / FF, sq_b, rr2,
                    dep=tok_hev[127])

    tok_gev = {}
    tok_dnh = {}
    t_sg_last = None
    wbufd_h = wbufd[:, :, :].rearrange("p (hh c) n -> p hh c n", hh=2)
    for dc in range(8):
        if dc == 0:
            wt(GP, tok_upmm[31])
        if (dc - 1, 0) in tok_dnh:
            wt(GP, tok_dnh[(dc - 1, 0)])
        twl0 = gpdma(wbufd_h[:, 0, :, :],
                     wdT_d[dc, :, 0:2048].rearrange("p (c n) -> p c n", c=16),
                     sem=su[0])
        if (dc - 1, 1) in tok_dnh:
            wt(GP, tok_dnh[(dc - 1, 1)])
        twl1 = gpdma(wbufd_h[:, 1, :, :],
                     wdT_d[dc, :, 2048:4096].rearrange("p (c n) -> p c n", c=16),
                     sem=su[1])
        for g in range(4):
            i = dc * 4 + g
            pb = i % 2
            sps = PS[:, 512 * pb:512 * (pb + 1)]
            wt(PE, twl0)
            wt(PE, t_r2)
            if (i - 2) in tok_gev:
                wt(PE, tok_gev[i - 2])
            tm = None
            for fc in range(16):
                tm = inc(PE.matmul(sps, wbufd_h[:, 0, fc, :],
                                   h_fm[:, fc, 512 * g:512 * (g + 1)],
                                   start=(fc == 0), stop=False), s_pe)
            tok_dnh[(dc, 0)] = tm
            wt(PE, twl1)
            for fc in range(16, 32):
                tm = inc(PE.matmul(sps, wbufd_h[:, 1, fc - 16, :],
                                   h_fm[:, fc, 512 * g:512 * (g + 1)],
                                   start=False, stop=(fc == 31)), s_pe)
            tok_dnh[(dc, 1)] = tm
            wt(DVE, tm)
            tgp = inc(DVE.tensor_tensor(gpre[:, pb, :], sps,
                                        r_bc[:, 512 * g:512 * (g + 1)], op=ALU.mult),
                      s_dve)
            wt(ACT, tgp)
            tsi = inc(ACT.activation(gpre[:, pb, :], gpre[:, pb, :], AF.Silu), s_act)
            wt(DVE, tsi)
            tok_gev[i] = t_sg_last = inc(
                DVE.tensor_tensor(s_bf[:, dc, 512 * g:512 * (g + 1)],
                                  gpre[:, pb, :],
                                  s_bf[:, dc, 512 * g:512 * (g + 1)], op=ALU.mult),
                s_dve)

    nc.all_engine_barrier()

    tok_add = [None, None]
    tok_store = [None, None]
    for tix in range(16):
        sl = tix % 2
        if tok_store[sl] is not None:
            wt(SP, tok_store[sl])
        t_x = spdma(xo_ring[:, sl, :], xo_d[128 * tix:128 * (tix + 1), :], sem=so[sl])
        pbf = bankb(3 if sl == 0 else 5)
        if tok_add[sl] is not None:
            wt(PE, tok_add[sl])
        ttp = None
        for c in range(8):
            ttp = inc(PE.transpose(pbf[:, 128 * c:128 * (c + 1)],
                                   s_bf[:, c, 128 * tix:128 * (tix + 1)],
                                   identB[:, :]), s_pe)
        wt(DVE, ttp)
        wt(DVE, t_x)
        tok_add[sl] = inc(DVE.tensor_tensor(xo_ring[:, sl, :], pbf[:, :],
                                            xo_ring[:, sl, :], op=ALU.add), s_dve)
        wt(SP, tok_add[sl])
        tok_store[sl] = spdma(out_d[128 * tix:128 * (tix + 1), :], xo_ring[:, sl, :],
                              sem=so[sl])

    SP.wait_ge(so[0], cnt[so[0]])
    SP.wait_ge(so[1], cnt[so[1]])
    nc.all_engine_barrier()
    return nc, dbg



def kernel(L0_attn_x, L0_attn_w_qkv, L0_attn_w_out, L0_attn_b_out,
           L0_ffn_w_up, L0_ffn_w_down):
    x = np.ascontiguousarray(np.asarray(L0_attn_x, dtype=np.float32))
    wqkv = np.ascontiguousarray(np.asarray(L0_attn_w_qkv, dtype=np.float32))
    wo = np.ascontiguousarray(np.asarray(L0_attn_w_out, dtype=np.float32))
    bo = np.ascontiguousarray(np.asarray(L0_attn_b_out, dtype=np.float32)).reshape(8, 128)
    wu = np.ascontiguousarray(np.asarray(L0_ffn_w_up, dtype=np.float32))
    wd = np.ascontiguousarray(np.asarray(L0_ffn_w_down, dtype=np.float32))

    nc, _ = build_program()
    in_maps = []
    for c in range(8):
        b = c // 4
        r = c % 4
        h0 = 4 * r
        in_maps.append({
            "x": np.ascontiguousarray(x[b]),
            "xo": np.ascontiguousarray(x[b, NO * r:NO * (r + 1)]),
            "wq": np.ascontiguousarray(wqkv[:, 64 * h0:64 * h0 + 256]),
            "wk": np.ascontiguousarray(wqkv[:, 1024 + 64 * h0:1024 + 64 * h0 + 256]),
            "wv": np.ascontiguousarray(wqkv[:, 2048 + 64 * h0:2048 + 64 * h0 + 256]),
            "wo": wo,
            "bo": bo,
            "wu": wu,
            "wd": wd,
        })
    res = run_bass_kernel_spmd(nc, in_maps, core_ids=list(range(8)))
    outs = [res.results[c]["out"] for c in range(8)]
    full = np.concatenate(outs, axis=0).reshape(2, N, D)
    return full


# version 08
# speedup vs baseline: 1.1492x; 1.1145x over previous
import numpy as np
import concourse.bass as bass
import concourse.mybir as mybir
from concourse.bass_utils import run_bass_kernel_spmd

F32 = mybir.dt.float32
BF16 = mybir.dt.bfloat16
AF = mybir.ActivationFunctionType
ALU = mybir.AluOpType
AX = mybir.AxisListType

PI = float(np.pi)
MAGIC = 12582912.0
INV2PI = float(np.float64(1.0) / (2.0 * np.pi))
CW1 = float(np.float32(6.28125))
CW2 = float(np.float32(2.0 * np.pi - 6.28125))
CW3 = float(2.0 * np.pi - 6.28125 - float(np.float32(2.0 * np.pi - 6.28125)))
LN1E4_32 = float(np.log(10000.0) / 32.0)
LN0125 = float(np.log(0.125))

N = 8192
D = 1024
NT = 64
NO = 2048
FF = 4096
DEBUG = False
PHASES = 6
SIM_REDUCE = False


def build_program():
    nc = bass.Bass("TRN2", target_bir_lowering=False, debug=False)

    x_d = nc.dram_tensor("x", [N, D], F32, kind="ExternalInput").ap()
    xo_d = nc.dram_tensor("xo", [NO, D], F32, kind="ExternalInput").ap()
    wq_d = nc.dram_tensor("wq", [D, 256], F32, kind="ExternalInput").ap()
    wk_d = nc.dram_tensor("wk", [D, 256], F32, kind="ExternalInput").ap()
    wv_d = nc.dram_tensor("wv", [D, 256], F32, kind="ExternalInput").ap()
    wo_d = nc.dram_tensor("wo", [D, D], F32, kind="ExternalInput").ap()
    bo_d = nc.dram_tensor("bo", [8, 128], F32, kind="ExternalInput").ap()
    wu_d = nc.dram_tensor("wu", [FF, D], F32, kind="ExternalInput").ap()
    wd_d = nc.dram_tensor("wd", [D, FF], F32, kind="ExternalInput").ap()
    out_d = nc.dram_tensor("out", [NO, D], F32, kind="ExternalOutput").ap()

    cc_in = nc.dram_tensor("cc_in", [256, N], BF16, kind="Internal")
    cc_out = nc.dram_tensor("cc_out", [1024, N], BF16, kind="Internal")
    wuT_d = nc.dram_tensor("wuT", [32, 128, D], BF16, kind="Internal").ap()
    wdT_d = nc.dram_tensor("wdT", [8, 128, FF], BF16, kind="Internal").ap()

    dbg = {}
    if DEBUG:
        def dob(nm, sh):
            dbg[nm] = nc.dram_tensor("dbg_" + nm, sh, F32, kind="ExternalOutput").ap()
        dob("q", [128, 2 * N])
        dob("k", [128, 2 * N])
        dob("v", [128, 2 * 160])
        dob("o", [128, 2 * N])
        dob("s", [128, 8 * 512])
        dob("h", [128, 2 * NO])
        dob("m", [128, 8])

    base = (nc.sbuf_base + 31) // 32 * 32
    AVAIL = nc.SBUF_PARTITION_SIZE_BYTES - base

    def at(name, shape, dtype, off):
        assert off % 32 == 0, (name, off)
        return nc.alloc_sbuf_tensor_at(name, shape, dtype, offset=base + off).ap()

    U0, W0 = 0, 131072
    P0 = W0 + 45056
    S0 = P0 + 32768
    TOTAL = S0 + 3840
    assert TOTAL <= AVAIL, (TOTAL, AVAIL)

    q_dm = at("q_dm", [128, 2, N], BF16, U0)
    k_dm = at("k_dm", [128, 2, N], BF16, U0 + 32768)
    v_tm = at("v_tm", [128, NT, 2, 160], BF16, U0 + 65536)
    cos_t = at("cos_t", [128, NT, 32], BF16, U0 + 106496)
    sin_t = at("sin_t", [128, NT, 32], BF16, U0 + 110592)
    afin = at("afin", [1, 1040], F32, U0 + 122880)
    wf_ring = at("wf_ring", [128, 2, 1024], F32, P0 + 0)
    tern_scr = at("tern_scr", [128, 2, 2, 1024], BF16, P0 + 8192)
    wspill = at("wspill", [128, 2, 1024], BF16, P0 + 16384)
    absw_scr = at("absw_scr", [128, 2, 1024], BF16, P0 + 20480)
    o_own = at("o_own", [128, 8, NO], BF16, U0 + 0)
    wout_scr = at("wout_scr", [128, 4, 1024], F32, U0 + 32768)
    h_fm = at("h_fm", [128, 32, NO], BF16, U0 + 0)

    xin = at("xin", [128, 2, 1024], F32, W0 + 0)
    xln = at("xln", [128, 2, 1024], BF16, W0 + 8192)
    xnE = at("xnE", [128, 2, 8, 128], BF16, W0 + 12288)
    qkr = at("qkr", [128, 2, 512], BF16, W0 + 16384)
    wqk = at("wqk", [128, 8, 512], BF16, W0 + 18432)
    wv_s = at("wv_s", [128, 8, 256], BF16, W0 + 26624)
    rp_s = at("rp_s", [128, 4, 256], F32, W0 + 30720)
    th64 = at("th64", [1, 2048], F32, W0 + 34816)
    qkc = at("qkc", [128, 2, 512], BF16, W0 + 34816)
    rp_b = at("rp_b", [128, 2, 2, 256], BF16, W0 + 36864)
    tth = at("tth", [1, 2048], F32, W0 + 0)
    prow = at("prow", [1, 128], F32, W0 + 43008)
    pt_ring = at("pt_ring", [128, 3, 1024], BF16, W0 + 0)
    rcps = at("rcps", [128, 512], F32, W0 + 6144)
    qn_scr = at("qn_scr", [128, 512], BF16, W0 + 8192)
    runmax = at("runmax", [128, 4], F32, W0 + 9216)
    mrow = at("mrow", [128, 16], F32, W0 + 9280)
    wout_bf = at("wout_bf", [128, 8, 1024], BF16, U0 + 65536)
    xn_own = at("xn_own", [128, 8, NO], BF16, W0 + 0)
    gpre = at("gpre", [128, 2, 512], F32, W0 + 0)
    rr2 = at("rr2", [128, 512], F32, W0 + 16384)
    wbufd = at("wbufd", [128, 32, 128], BF16, W0 + 4096)
    xo_ring = at("xo_ring", [128, 2, 1024], F32, W0 + 16384)
    sq_b = at("sq_b", [128, 2048], BF16, W0 + 24576)
    r_bc = at("r_bc", [128, 2048], F32, W0 + 32768)
    wbufu = at("wbufu", [128, 2, 8, 128], BF16, W0 + 40960)
    sq_a = at("sq_a", [128, 2048], BF16, W0 + 40960)
    xin_d = at("xin_d", [128, 2, 1024], F32, U0 + 49152)
    xln_d = at("xln_d", [128, 2, 1024], BF16, U0 + 57344)
    bn_d = at("bn_d", [128, 32], F32, U0 + 61440)
    rr1 = at("rr1", [128, 512], F32, U0 + 61568)

    o_all = at("o_all", [128, 2, N], BF16, P0)
    s_bf = at("s_bf", [128, 8, NO], BF16, P0)

    so = [S0]

    def small(name, shape, dtype):
        nbytes = int(np.prod(shape[1:])) * (2 if dtype == BF16 else 4)
        off = so[0]
        so[0] = (off + nbytes + 31) // 32 * 32
        return at(name, shape, dtype, off)

    identB = small("identB", [128, 128], BF16)
    identF = small("identF", [128, 128], F32)
    onesP = small("onesP", [128, 128], F32)
    pA = small("pA", [128, 128], F32)
    pB = small("pB", [128, 128], F32)
    ind0 = small("ind0", [128, 128], BF16)
    ind64 = small("ind64", [128, 128], BF16)
    mhalf = small("mhalf", [128, 128], BF16)
    msum = small("msum", [128, 1], F32)
    ones1 = small("ones1", [128, 1], BF16)
    mbias = small("mbias", [128, 2], F32)
    epscol = small("epscol", [128, 1], F32)
    l0125 = small("l0125", [128, 1], F32)
    thru = small("thru", [128, 1], F32)
    nthru = small("nthru", [128, 1], F32)
    thrd = small("thrd", [128, 1], F32)
    nthrd = small("nthrd", [128, 1], F32)
    bo_sb = small("bo_sb", [128, 8], F32)
    assert so[0] - S0 <= 4096

    PS = nc.alloc_psum_tensor("PS", [128, 4096], F32).ap()

    def bankb(b, n=1):
        return PS[:, 512 * b:512 * (b + n)].bitcast(BF16)

    s_sp = nc.alloc_semaphore("s_sp")
    s_gp = nc.alloc_semaphore("s_gp")
    s_pe = nc.alloc_semaphore("s_pe")
    s_act = nc.alloc_semaphore("s_act")
    s_dve = nc.alloc_semaphore("s_dve")
    s_pool = nc.alloc_semaphore("s_pool")
    s_cc = nc.alloc_semaphore("s_cc")
    sx = [nc.alloc_semaphore("sx0"), nc.alloc_semaphore("sx1")]
    sw = [nc.alloc_semaphore("sw0"), nc.alloc_semaphore("sw1")]
    ssp1 = nc.alloc_semaphore("ssp1")
    ssp2 = nc.alloc_semaphore("ssp2")
    su = [nc.alloc_semaphore("su0"), nc.alloc_semaphore("su1")]
    sdd = nc.alloc_semaphore("sdd")
    so = [nc.alloc_semaphore("so0"), nc.alloc_semaphore("so1")]
    cnt = {s_sp: 0, s_gp: 0, s_pe: 0, s_act: 0, s_dve: 0, s_pool: 0, s_cc: 0,
           ssp1: 0, ssp2: 0, sdd: 0}
    for _s in sx + sw + su + so:
        cnt[_s] = 0

    def inc(inst, sem, d=1):
        inst.then_inc(sem, d)
        cnt[sem] += d
        return (sem, cnt[sem])

    def wt(eng, tok):
        if tok is not None:
            eng.wait_ge(tok[0], tok[1])

    SP, PE, ACT, DVE, GP = nc.sync, nc.tensor, nc.scalar, nc.vector, nc.gpsimd

    def spdma(dst, src, sem=None):
        return inc(SP.dma_start(out=dst, in_=src), sem or s_sp, 16)

    def gpdma(dst, src, sem=None):
        return inc(GP.dma_start(out=dst, in_=src), sem or s_gp, 16)

    GP.iota(prow[:, :], pattern=[[1, 128]], base=0, channel_multiplier=0,
            allow_small_or_imprecise_dtypes=True)
    GP.iota(th64[:, :], pattern=[[0, 64], [1, 32]], base=0, channel_multiplier=0,
            allow_small_or_imprecise_dtypes=True)
    GP.iota(tth[:, :], pattern=[[128, 64], [0, 32]], base=0, channel_multiplier=0,
            allow_small_or_imprecise_dtypes=True)
    def gpop(inst):
        tok = inc(inst, s_pool)
        GP.wait_ge(tok[0], tok[1])
        return tok
    for ap_ in (identB, identF):
        gpop(GP.memset(ap_[:, :], 0.0))
        gpop(GP.affine_select(out=ap_[:, :], in_=ap_[:, :], compare_op=ALU.not_equal,
                              fill=1.0, base=0, pattern=[[-1, 128]], channel_multiplier=1))
    gpop(GP.memset(onesP[:, :], 0.0))
    gpop(GP.memset(onesP[0:1, :], 1.0))
    gpop(GP.memset(onesP[64:65, :], 1.0))
    gpop(GP.memset(pA[:, :], 0.0))
    gpop(GP.memset(pA[64:65, 0:64], 1.0))
    gpop(GP.memset(pB[:, :], 0.0))
    gpop(GP.memset(pB[32:33, 64:128], 1.0))
    gpop(GP.memset(ind0[:, :], 0.0))
    gpop(GP.memset(ind0[:, 0:1], 1.0))
    gpop(GP.memset(ind64[:, :], 0.0))
    gpop(GP.memset(ind64[:, 64:65], 1.0))
    gpop(GP.memset(mhalf[:, :], 0.0))
    gpop(GP.memset(mhalf[0:64, 0:1], 1.0))
    gpop(GP.memset(mhalf[64:128, 64:65], 1.0))
    gpop(GP.memset(msum[:, :], 0.0))
    gpop(GP.memset(msum[0:1, :], 1.0))
    gpop(GP.memset(msum[64:65, :], 1.0))
    gpop(GP.memset(ones1[:, :], 1.0))
    gpop(GP.memset(epscol[:, :], 1e-5))
    gpop(GP.memset(l0125[:, :], LN0125))
    gpop(GP.memset(v_tm[:, :, :, :], 0.0))
    t_cst = gpop(GP.memset(v_tm[:, :, :, 64:65], 1.0))

    wt(ACT, t_cst)
    t_th = inc(ACT.activation(th64[:, :], th64[:, :], AF.Exp, scale=-LN1E4_32), s_act)
    wt(DVE, t_th)
    t_tth = inc(DVE.tensor_mul(tth[:, :], tth[:, :], th64[:, :]), s_dve)
    t_tabs = None
    for b in range(4):
        tb = PS[:, 2560:3072]
        red = rp_s[:, 0:2, :].rearrange("p a d -> p (a d)")
        scr = rp_s[:, 2:4, :].rearrange("p a d -> p (a d)")
        wt(PE, t_tth)
        if b > 0:
            wt(PE, (s_act, cnt[s_act]))
        PE.matmul(tb, prow[:, :], th64[:, 0:512], start=True, stop=False)
        t_ang = inc(PE.matmul(tb, onesP[0:1, :], tth[:, 512 * b:512 * (b + 1)],
                              start=False, stop=True), s_pe)
        wt(DVE, t_ang)
        t = inc(DVE.tensor_scalar(scr, tb, INV2PI, MAGIC, op0=ALU.mult, op1=ALU.add), s_dve)
        wt(DVE, t)
        t = inc(DVE.tensor_scalar_add(scr, scr, -MAGIC), s_dve)
        wt(DVE, t)
        t = inc(DVE.scalar_tensor_tensor(red, scr, -CW1, tb, op0=ALU.mult, op1=ALU.add), s_dve)
        wt(DVE, t)
        t = inc(DVE.scalar_tensor_tensor(red, scr, -CW2, red, op0=ALU.mult, op1=ALU.add), s_dve)
        wt(DVE, t)
        t = inc(DVE.scalar_tensor_tensor(red, scr, -CW3, red, op0=ALU.mult, op1=ALU.add), s_dve)
        wt(DVE, t)
        t = inc(DVE.tensor_scalar_min(red, red, PI), s_dve)
        wt(DVE, t)
        tred = inc(DVE.tensor_scalar_max(red, red, -PI), s_dve)
        wt(ACT, tred)
        tsin = inc(ACT.activation(
            sin_t[:, 16 * b:16 * (b + 1), :].rearrange("p a d -> p (a d)"),
            red, AF.Sin), s_act)
        wt(DVE, tred)
        t = inc(DVE.tensor_scalar_add(scr, red, PI / 2), s_dve)
        wt(DVE, t)
        wt(DVE, tsin)
        t = inc(DVE.tensor_single_scalar(red, scr, PI, op=ALU.is_gt), s_dve)
        wt(DVE, t)
        t = inc(DVE.scalar_tensor_tensor(scr, red, -2 * PI, scr, op0=ALU.mult, op1=ALU.add), s_dve)
        wt(ACT, t)
        wt(ACT, tsin)
        t_tabs = inc(ACT.activation(
            cos_t[:, 16 * b:16 * (b + 1), :].rearrange("p a d -> p (a d)"),
            scr, AF.Sin), s_act)

    def load_w256(dst_slice, src):
        toks = []
        for hh in range(2):
            tl = spdma(xin[:, hh, :].rearrange("p (c n) -> p c n", c=4),
                       src.rearrange("(c p) n -> p c n", p=128)[:, 4 * hh:4 * (hh + 1), :],
                       sem=sx[hh])
            toks.append(tl)
        wt(DVE, toks[0])
        wt(DVE, toks[1])
        t0 = inc(DVE.tensor_copy(dst_slice(0), xin[:, 0, :].rearrange("p (c n) -> p c n", c=4)), s_dve)
        t1 = inc(DVE.tensor_copy(dst_slice(1), xin[:, 1, :].rearrange("p (c n) -> p c n", c=4)), s_dve)
        wt(SP, t1)
        return t1

    wt(SP, (s_pe, cnt[s_pe]))
    wt(SP, t_tabs)
    load_w256(lambda hh: wqk[:, 4 * hh:4 * (hh + 1), 0:256], wq_d)
    t_wq = load_w256(lambda hh: wqk[:, 4 * hh:4 * (hh + 1), 256:512], wk_d)
    t_wv = load_w256(lambda hh: wv_s[:, 4 * hh:4 * (hh + 1), :], wv_d)

    wt(GP, (s_dve, cnt[s_dve]))
    t_bo = gpdma(rp_s[0:8, 0, 0:128], bo_d[:, :])
    wt(PE, t_bo)
    wt(PE, t_tabs)
    t = inc(PE.transpose(PS[0:128, 2560:2568], rp_s[0:8, 0, 0:128], identF[0:8, 0:8]), s_pe)
    wt(DVE, t)
    t_bosb = inc(DVE.tensor_copy(bo_sb[:, :], PS[:, 2560:2568]), s_dve)

    def blocks_up():
        return [(wu_d[128 * i:128 * (i + 1), :], wuT_d[i, :, :]) for i in range(32)]

    def blocks_dn():
        out = []
        for dcq in range(32):
            dc, q4 = dcq // 4, dcq % 4
            out.append((wd_d[128 * dc:128 * (dc + 1), 1024 * q4:1024 * (q4 + 1)],
                        wdT_d[dc, :, 1024 * q4:1024 * (q4 + 1)]))
        return out

    AB7 = PS[0:1, 3584:4096]

    def alpha_jobs(blocks, state, dep):
        nb = len(blocks)
        for i, (src, _) in enumerate(blocks):
            def load(i=i, src=src):
                sl = i % 2
                if state.get("abs%d" % sl) is not None:
                    wt(GP, state["abs%d" % sl])
                elif state.get("war") is not None:
                    wt(GP, state["war"])
                state["ld%d" % sl] = gpdma(wf_ring[:, sl, :], src, sem=sw[sl])

            def comp(i=i):
                sl = i % 2
                if state.get("mm%d" % sl) is not None:
                    wt(ACT, state["mm%d" % sl])
                wt(ACT, state["ld%d" % sl])
                wt(ACT, dep)
                if i == 0 and state.get("war") is not None:
                    wt(ACT, state["war"])
                state["abs%d" % sl] = state["abs"] = inc(
                    ACT.activation(absw_scr[:, sl, :], wf_ring[:, sl, :], AF.Abs),
                    s_act)
                wt(PE, state["abs"])
                if i == 0 and state.get("war") is not None:
                    wt(PE, state["war"])
                PE.matmul(AB7, ones1[:, :], absw_scr[:, sl, 512:1024],
                          start=(i == 0), stop=False)
                state["mm%d" % sl] = state["mm"] = inc(
                    PE.matmul(AB7, ones1[:, :], absw_scr[:, sl, 0:512],
                              start=False, stop=(i == nb - 1)), s_pe)
            yield (load, comp)

    def alpha_fin(state, nb, thr_ap, nthr_ap):
        tok_mm = state["mm"]
        wt(DVE, tok_mm)
        t = inc(DVE.tensor_copy(afin[0:1, 0:512], AB7[0:1, 0:512]), s_dve)
        wt(DVE, t)
        t = inc(DVE.reduce_sum(afin[0:1, 1024:1025], afin[0:1, 0:512], axis=AX.X),
                s_dve)
        wt(DVE, t)
        t = inc(DVE.tensor_scalar_mul(afin[0:1, 1024:1025], afin[0:1, 1024:1025],
                                      0.5 / (nb * 128 * 1024)), s_dve)
        wt(PE, t)
        wt(PE, tok_mm)
        t = inc(PE.matmul(PS[0:128, 3584:3585], onesP[0:1, :], afin[0:1, 1024:1025],
                          start=True, stop=True), s_pe)
        wt(DVE, t)
        t = inc(DVE.tensor_copy(thr_ap[:, :], PS[0:128, 3584:3585]), s_dve)
        wt(DVE, t)
        t = inc(DVE.tensor_scalar_mul(nthr_ap[:, :], thr_ap[:, :], -1.0), s_dve)
        state["thr"] = t
        return t

    def spill_jobs(blocks, state):
        for i, (src, dst, thr_ap, nthr_ap, dep_key) in enumerate(blocks):
            def load(i=i, src=src, dep_key=dep_key):
                sl = i % 2
                if state.get("tern%d" % sl) is not None:
                    wt(GP, state["tern%d" % sl])
                if i < 2:
                    wt(GP, state[dep_key])
                state["ld%d" % sl] = gpdma(wf_ring[:, sl, :], src, sem=sw[sl])

            def comp(i=i, dst=dst, thr_ap=thr_ap, nthr_ap=nthr_ap,
                     dep_key=dep_key):
                sl = i % 2
                wt(DVE, state["ld%d" % sl])
                wt(DVE, state[dep_key])
                if state.get("T%d" % sl) is not None:
                    wt(DVE, state["T%d" % sl])
                t = inc(DVE.tensor_scalar(tern_scr[:, sl, 0, :], wf_ring[:, sl, :],
                                          thr_ap[:, :], None, op0=ALU.is_gt),
                        s_dve)
                wt(DVE, t)
                t = inc(DVE.tensor_scalar(tern_scr[:, sl, 1, :], wf_ring[:, sl, :],
                                          nthr_ap[:, :], None, op0=ALU.is_lt),
                        s_dve)
                wt(DVE, t)
                t_wt = inc(DVE.tensor_sub(tern_scr[:, sl, 0, :],
                                          tern_scr[:, sl, 0, :],
                                          tern_scr[:, sl, 1, :]), s_dve)
                state["tern%d" % sl] = state["tern"] = t_wt
                pb = bankb(6)
                wt(PE, t_wt)
                if state.get("ev") is not None:
                    wt(PE, state["ev"])
                ttp = None
                for ci in range(8):
                    ttp = inc(PE.transpose(pb[:, 128 * ci:128 * (ci + 1)],
                                           tern_scr[:, sl, 0, 128 * ci:128 * (ci + 1)],
                                           identB[:, :]), s_pe)
                state["T%d" % sl] = ttp
                wt(ACT, ttp)
                if state.get("sp%d" % sl) is not None:
                    wt(ACT, state["sp%d" % sl])
                tok_ev = inc(ACT.activation(wspill[:, sl, :], pb[:, :],
                                            AF.Identity), s_act)
                state["ev"] = tok_ev
                wt(GP, tok_ev)
                state["sp%d" % sl] = gpdma(dst, wspill[:, sl, :],
                                           sem=(ssp1, ssp2)[sl])
            yield (load, comp)

    st_au = {}
    st_ad = {}

    st_sp = {}

    def fin_up_job():
        alpha_fin(st_au, 32, thru, nthru)
        st_sp["thr_up"] = st_au["thr"]
        st_ad["war"] = st_au["thr"]

    def fin_dn_job():
        alpha_fin(st_ad, 32, thrd, nthrd)
        st_sp["thr_dn"] = st_ad["thr"]

    spill_blocks = ([(s, d, thru, nthru, "thr_dn") for (s, d) in blocks_up()] +
                    [(s, d, thrd, nthrd, "thr_dn") for (s, d) in blocks_dn()])
    wjobs = (list(alpha_jobs(blocks_up(), st_au, t_bosb)) + [(None, fin_up_job)] +
             list(alpha_jobs(blocks_dn(), st_ad, t_bosb)) + [(None, fin_dn_job)] +
             list(spill_jobs(spill_blocks, st_sp)))
    wjob_i = [0]
    wload_i = [0]

    def run_wjobs(target):
        tgt = min(target, len(wjobs))
        while wjob_i[0] < tgt:
            while wload_i[0] < min(wjob_i[0] + 2, len(wjobs)):
                ld = wjobs[wload_i[0]][0]
                if ld is None:
                    if wload_i[0] >= wjob_i[0]:
                        break
                    wload_i[0] += 1
                    continue
                ld()
                wload_i[0] += 1
            wjobs[wjob_i[0]][1]()
            wjob_i[0] += 1

    SP.wait_ge(sx[0], cnt[sx[0]])
    SP.wait_ge(sx[1], cnt[sx[1]])
    nc.all_engine_barrier()
    if PHASES <= 1:
        return nc, dbg

    def emit_ln_tile(src_rows, sl, dep_war_xin, dep_war_xln, xin_ap, xln_ap, bnp,
                     t_x):
        bn_ap = bnp[:, 16 * sl:16 * (sl + 1)]
        wt(DVE, t_x)
        t1 = inc(DVE.bn_stats(bn_ap[:, 0:6], xin_ap[:, sl, 0:512]), s_dve)
        t2 = inc(DVE.bn_stats(bn_ap[:, 6:12], xin_ap[:, sl, 512:1024]), s_dve)
        wt(DVE, t2)
        t3 = inc(DVE.bn_aggr(bn_ap[:, 12:14], bn_ap[:, 0:12]), s_dve)
        wt(ACT, t3)
        t4 = inc(ACT.activation(bn_ap[:, 14:15], bn_ap[:, 13:14], AF.Ln,
                                bias=epscol[:, :]), s_act)
        wt(ACT, t4)
        t5 = inc(ACT.activation(bn_ap[:, 14:15], bn_ap[:, 14:15], AF.Exp,
                                scale=-0.5), s_act)
        wt(DVE, t5)
        t5b = inc(DVE.scalar_tensor_tensor(bn_ap[:, 15:16], bn_ap[:, 12:13], -1.0,
                                           bn_ap[:, 14:15], op0=ALU.mult,
                                           op1=ALU.mult), s_dve)
        wt(ACT, t5b)
        if dep_war_xln is not None:
            wt(ACT, dep_war_xln)
        t6 = inc(ACT.activation(xln_ap[:, sl, :], xin_ap[:, sl, :], AF.Identity,
                                bias=bn_ap[:, 15:16], scale=bn_ap[:, 14:15]),
                 s_act)
        return t6

    def ln_dma(src_rows, sl, dep_war_xin, xin_ap):
        if dep_war_xin is not None:
            wt(SP, dep_war_xin)
        return spdma(xin_ap[:, sl, :], src_rows, sem=sx[sl])

    bn_a = rp_s[:, 0, 0:32]
    VBANK = [PS[:, 1024:1280], PS[:, 2560:2816]]
    tk = [dict() for _ in range(NT)]

    def st1(t):
        sl = t % 2
        dep_xln = tk[t - 2]["T8"] if t >= 2 else None
        tk[t]["xn"] = emit_ln_tile(None, sl, None, dep_xln, xin, xln, bn_a,
                                   tk[t]["xdma"])

    def st2(t):
        sl = t % 2
        pbx = bankb(3 + sl)
        wt(PE, tk[t]["xn"])
        if t >= 3:
            wt(PE, tk[t - 3]["qkev"])
        ttp = None
        for c in range(8):
            ttp = inc(PE.transpose(pbx[:, 128 * c:128 * (c + 1)],
                                   xln[:, sl, 128 * c:128 * (c + 1)],
                                   identB[:, :]), s_pe)
        tk[t]["T8"] = ttp
        wt(DVE, ttp)
        if t >= 2:
            wt(DVE, tk[t - 2]["qk"])
        tk[t]["xnE"] = inc(
            DVE.tensor_copy(xnE[:, sl, :, :],
                            pbx[:, :].rearrange("p (c n) -> p c n", c=8)), s_dve)
        wt(PE, tk[t]["xnE"])
        wt(PE, t_wv)
        if t >= 2:
            wt(PE, tk[t - 2]["vev"])
        tvm = None
        for c in range(8):
            tvm = inc(PE.matmul(VBANK[sl], xnE[:, sl, c, :], wv_s[:, c, :],
                                start=(c == 0), stop=(c == 7)), s_pe)
        tk[t]["vmm"] = tvm
        wt(PE, t_wq)
        if t >= 2:
            wt(PE, tk[t - 2]["qkc"])
        tqk = None
        for c in range(8):
            tqk = inc(PE.matmul(PS[:, 512 * sl:512 * (sl + 1)], xnE[:, sl, c, :],
                                wqk[:, c, :], start=(c == 0), stop=(c == 7)),
                      s_pe)
        tk[t]["qk"] = tqk
        wt(ACT, tvm)
        vdst = bass.AP(v_tm.tensor, t * 320,
                       [[20480, 128], [160, 2], [96, 2], [1, 64]])
        tk[t]["vev"] = inc(ACT.activation(
            vdst, VBANK[sl].rearrange("p (a h d) -> p a h d", a=2, h=2),
            AF.Identity), s_act)
        wt(ACT, tqk)
        wt(ACT, t_tabs)
        if t >= 2:
            wt(ACT, tk[t - 2]["rope"])
        tk[t]["qkc"] = inc(ACT.activation(qkc[:, sl, :],
                                          PS[:, 512 * sl:512 * (sl + 1)],
                                          AF.Identity), s_act)

    def st3a(t):
        sl = t % 2
        qv = qkc[:, sl, :].rearrange("p (h d) -> p h d", d=64)
        x1 = qv[:, :, 0:32]
        x2 = qv[:, :, 32:64]
        cosv = cos_t[:, t, :].unsqueeze(1).broadcast_to([128, 8, 32])
        sinv = sin_t[:, t, :].unsqueeze(1).broadcast_to([128, 8, 32])
        r3 = rp_b[:, sl, 0, :].rearrange("p (a d) -> p a d", d=32)[:, 0:8, :]
        r4 = rp_b[:, sl, 1, :].rearrange("p (a d) -> p a d", d=32)[:, 0:8, :]
        o1 = qkr[:, sl, :].rearrange("p (h d) -> p h d", d=64)[:, :, 0:32]
        o2 = qkr[:, sl, :].rearrange("p (h d) -> p h d", d=64)[:, :, 32:64]
        wt(DVE, tk[t]["qkc"])
        wt(DVE, t_tabs)
        if t >= 2:
            wt(DVE, tk[t - 2]["qkT"])
        DVE.tensor_tensor(r3, x1, cosv, op=ALU.mult)
        tr = inc(DVE.tensor_tensor(r4, x2, sinv, op=ALU.mult), s_dve)
        wt(DVE, tr)
        tr = inc(DVE.tensor_sub(o1, r3, r4), s_dve)
        wt(DVE, tr)
        DVE.tensor_tensor(r3, x1, sinv, op=ALU.mult)
        tr = inc(DVE.tensor_tensor(r4, x2, cosv, op=ALU.mult), s_dve)
        wt(DVE, tr)
        tk[t]["rope"] = inc(DVE.tensor_add(o2, r3, r4), s_dve)

    def st3b(t):
        sl = t % 2
        pbq = bankb(3 + (1 - sl))
        wt(PE, tk[t]["rope"])
        wt(PE, tk[t + 1]["xnE"] if t + 1 < NT else tk[t - 1]["xnE"])
        if t + 1 >= NT and t >= 2:
            wt(PE, tk[t - 2]["qkev"])
        ttq = None
        for c in range(4):
            ttq = inc(PE.transpose(pbq[:, 128 * c:128 * (c + 1)],
                                   qkr[:, sl, 128 * c:128 * (c + 1)],
                                   identB[:, :]), s_pe)
        tk[t]["qkT"] = ttq
        wt(ACT, ttq)
        ACT.activation(q_dm[:, :, 128 * t:128 * (t + 1)],
                       pbq[:, 0:256].rearrange("p (a n) -> p a n", a=2),
                       AF.Identity)
        tk[t]["qkev"] = inc(
            ACT.activation(k_dm[:, :, 128 * t:128 * (t + 1)],
                           pbq[:, 256:512].rearrange("p (a n) -> p a n", a=2),
                           AF.Identity), s_act)

    tk[0]["xdma"] = ln_dma(x_d[0:128, :], 0, None, xin)
    for b in range(NT + 3):
        if b + 1 < NT:
            slp = (b + 1) % 2
            dep = tk[b - 1]["xn"] if b >= 1 else None
            tk[b + 1]["xdma"] = ln_dma(x_d[128 * (b + 1):128 * (b + 2), :],
                                       slp, dep, xin)
        if b < NT:
            st1(b)
        if 1 <= b <= NT:
            st2(b - 1)
        if 2 <= b <= NT + 1:
            st3a(b - 2)
        if 3 <= b <= NT + 2:
            st3b(b - 3)
        run_wjobs((b + 1) * 131 // 64)

    run_wjobs(len(wjobs))
    GP.wait_ge(ssp1, cnt[ssp1])
    GP.wait_ge(ssp2, cnt[ssp2])
    GP.wait_ge(sw[0], cnt[sw[0]])
    GP.wait_ge(sw[1], cnt[sw[1]])
    GP.wait_ge(s_gp, cnt[s_gp])
    nc.all_engine_barrier()

    if DEBUG:
        gpdma(dbg["q"].rearrange("p (a n) -> p a n", a=2), q_dm[:, :, :])
        gpdma(dbg["k"].rearrange("p (a n) -> p a n", a=2), k_dm[:, :, :])
        gpdma(dbg["v"].rearrange("p (a n) -> p a n", a=2), v_tm[:, 0, :, :])
        GP.wait_ge(s_gp, cnt[s_gp])
        nc.all_engine_barrier()
    if PHASES <= 2:
        return nc, dbg

    tok_run = inc(DVE.memset(runmax[:, :], 0.0), s_dve)
    tok_rmax = None
    for j, src in ((0, q_dm), (1, k_dm)):
        for pr in range(2):
            for g in range(16):
                sl512 = src[:, pr, 512 * g:512 * (g + 1)]
                wt(DVE, tok_rmax)
                tsq = inc(DVE.tensor_mul(qn_scr[:, :], sl512, sl512), s_dve)
                wt(PE, tsq)
                if tok_rmax is not None:
                    wt(PE, tok_rmax)
                tmm = inc(PE.matmul(PS[:, 1536:2048], mhalf[:, :], qn_scr[:, :],
                                    start=True, stop=True), s_pe)
                wt(DVE, tmm)
                tr1 = inc(DVE.reduce_max(rp_s[:, 3, 0:1], PS[:, 1536:2048], axis=AX.X),
                          s_dve)
                wt(DVE, tr1)
                wt(DVE, tok_run)
                ji = 2 * j + pr
                tok_run = inc(DVE.tensor_max(runmax[:, ji:ji + 1],
                                             runmax[:, ji:ji + 1],
                                             rp_s[:, 3, 0:1]), s_dve)
                tok_rmax = tok_run
    wt(DVE, tok_run)
    t = inc(DVE.tensor_mul(runmax[:, 0:2], runmax[:, 0:2], runmax[:, 2:4]), s_dve)
    wt(PE, t)
    t = inc(PE.matmul(PS[0:1, 1536:1538], msum[:, :], runmax[:, 0:2],
                      start=True, stop=True), s_pe)
    wt(DVE, t)
    t = inc(DVE.tensor_copy(mrow[0:1, 0:2], PS[0:1, 1536:1538]), s_dve)
    wt(ACT, t)
    t = inc(ACT.activation(mrow[0:1, 2:4], mrow[0:1, 0:2], AF.Ln), s_act)
    wt(ACT, t)
    t = inc(ACT.activation(mrow[0:1, 4:6], mrow[0:1, 2:4], AF.Exp, scale=0.5,
                           bias=l0125[0:1, :]), s_act)
    wt(DVE, t)
    t = inc(DVE.tensor_scalar_mul(mrow[0:1, 6:8], mrow[0:1, 4:6], -1.0), s_dve)
    wt(PE, t)
    t = inc(PE.matmul(PS[:, 1536:1538], onesP[0:1, :], mrow[0:1, 6:8],
                      start=True, stop=True), s_pe)
    wt(DVE, t)
    t_mb = inc(DVE.tensor_copy(mbias[:, :], PS[:, 1536:1538]), s_dve)

    nc.all_engine_barrier()
    if PHASES <= 3:
        return nc, dbg

    if SIM_REDUCE:
        GP.memset(o_all[:, :, :], 0.0)
        nc.all_engine_barrier()
    NQ = 16
    NKT = 64
    b_pairs = [0] if PHASES in (3.5, 3.7) else range(2)
    b_qgs = [0] if PHASES == 3.5 else range(NQ)
    if SIM_REDUCE:
        b_pairs = range(2)
        b_qgs = [0, 1]
    tok_exp = {}
    tok_pv = {}
    tok_sc = {}
    stage_extra = {}
    prev_norm_tok = None

    def emit_pv(m_, pr_, first, last):
        kt_ = m_ % NKT
        stp = m_ % 3
        wt(PE, tok_exp[m_])
        if first and prev_norm_tok is not None:
            wt(PE, prev_norm_tok)
        PE.matmul(PS[0:65, 3072:3584], v_tm[:, kt_, pr_, 0:65],
                  pt_ring[:, stp, 0:512], start=first, stop=last)
        tok_pv[m_] = inc(
            PE.matmul(PS[:, 3584:4096], v_tm[:, kt_, pr_, 32:160],
                      pt_ring[:, stp, 512:1024], start=first, stop=last), s_pe)

    for pr in b_pairs:
        for qg in b_qgs:
            qslice = slice(512 * qg, 512 * (qg + 1))
            for kt in range(NKT):
                m = (pr * NQ + qg) * NKT + kt
                st = m % 3
                sb_ = PS[:, 1024 * st:1024 * (st + 1)]
                if (m - 3) in tok_exp:
                    wt(PE, tok_exp[m - 3])
                if st in stage_extra:
                    wt(PE, stage_extra.pop(st))
                PE.matmul(sb_[:, 0:512], k_dm[0:64, pr, 128 * kt:128 * (kt + 1)],
                          q_dm[0:64, pr, qslice], start=True, stop=True)
                tok_sc[m] = inc(
                    PE.matmul(sb_[:, 512:1024],
                              k_dm[64:128, pr, 128 * kt:128 * (kt + 1)],
                              q_dm[64:128, pr, qslice], start=True, stop=True), s_pe)
                wt(ACT, tok_sc[m])
                tok_exp[m] = inc(
                    ACT.activation(pt_ring[:, st, :], sb_[:, :], AF.Exp,
                                   bias=mbias[:, pr:pr + 1], scale=0.125), s_act)
                if kt > 1:
                    emit_pv(m - 2, pr, kt - 2 == 0, False)
            m_last = (pr * NQ + qg) * NKT + NKT - 1
            emit_pv(m_last - 1, pr, False, False)
            emit_pv(m_last, pr, False, True)
            wt(ACT, tok_pv[m_last])
            ta = inc(ACT.activation(rcps[64:65, :], PS[64:65, 3072:3584], AF.Ln), s_act)
            wt(ACT, ta)
            ta = inc(ACT.activation(rcps[64:65, :], rcps[64:65, :], AF.Exp,
                                    scale=-1.0), s_act)
            tb_ = inc(ACT.activation(rcps[32:33, :], PS[32:33, 3584:4096], AF.Ln), s_act)
            wt(ACT, tb_)
            tb_ = inc(ACT.activation(rcps[32:33, :], rcps[32:33, :], AF.Exp,
                                     scale=-1.0), s_act)
            rst = (m_last + 2) % 3
            rbankA = PS[:, 1024 * rst:1024 * rst + 512]
            rbankB = PS[:, 1024 * rst + 512:1024 * rst + 1024]
            wt(PE, ta)
            wt(PE, tb_)
            PE.matmul(rbankA, pA[64:65, :], rcps[64:65, :], start=True, stop=True)
            tR = inc(PE.matmul(rbankB, pB[32:33, :], rcps[32:33, :],
                               start=True, stop=True), s_pe)
            wt(DVE, tR)
            DVE.tensor_copy(rcps[0:64, :], rbankA[0:64, :])
            tc = inc(DVE.tensor_copy(rcps[64:128, :], rbankB[64:128, :]), s_dve)
            wt(DVE, tc)
            DVE.tensor_tensor(o_all[0:64, pr, qslice], PS[0:64, 3072:3584],
                              rcps[0:64, :], op=ALU.mult)
            prev_norm_tok = inc(
                DVE.tensor_tensor(o_all[64:128, pr, qslice], PS[64:128, 3584:4096],
                                  rcps[64:128, :], op=ALU.mult), s_dve)
            stage_extra[rst] = tc

    nc.all_engine_barrier()

    if DEBUG:
        gpdma(dbg["o"].rearrange("p (a n) -> p a n", a=2), o_all[:, :, :])
        GP.wait_ge(s_gp, cnt[s_gp])
        nc.all_engine_barrier()
    if PHASES <= 4.5:
        return nc, dbg

    def rms_to_R(get_chunk, nchunk, inv_d, sq_ap, rr_ap, dep=None):
        tok_mm = None
        tsq_prev = dep
        first = {4: True, 5: True}
        lastmm = {}
        for ci in range(nchunk):
            if tsq_prev is not None:
                wt(DVE, tsq_prev)
            tsq = inc(DVE.tensor_mul(sq_ap[:, :], get_chunk(ci), get_chunk(ci)), s_dve)
            wt(PE, tsq)
            for g in range(4):
                bk = PS[:, 512 * (4 + g // 2):512 * (5 + g // 2)]
                ind = ind0 if g % 2 == 0 else ind64
                b_ = 4 + g // 2
                tok_mm = inc(PE.matmul(bk, ind[:, :], sq_ap[:, 512 * g:512 * (g + 1)],
                                       start=first[b_],
                                       stop=(ci == nchunk - 1 and g % 2 == 1)), s_pe)
                first[b_] = False
            tsq_prev = tok_mm
        tlast = None
        for g in range(4):
            bk = PS[:, 512 * (4 + g // 2):512 * (5 + g // 2)]
            rr = 64 * (g % 2)
            wt(ACT, tok_mm)
            if tlast is not None:
                pass
            ta = inc(ACT.activation(rr_ap[rr:rr + 1, :], bk[rr:rr + 1, :], AF.Ln,
                                    scale=inv_d), s_act)
            wt(ACT, ta)
            ta = inc(ACT.activation(rr_ap[rr:rr + 1, :], rr_ap[rr:rr + 1, :], AF.Exp,
                                    scale=0.5), s_act)
            wt(DVE, ta)
            td = inc(DVE.tensor_scalar_add(rr_ap[rr:rr + 1, :], rr_ap[rr:rr + 1, :],
                                           1e-8), s_dve)
            wt(ACT, td)
            ta = inc(ACT.activation(rr_ap[rr:rr + 1, :], rr_ap[rr:rr + 1, :], AF.Ln), s_act)
            wt(ACT, ta)
            ta = inc(ACT.activation(rr_ap[rr:rr + 1, :], rr_ap[rr:rr + 1, :], AF.Exp,
                                    scale=-1.0), s_act)
            wt(PE, ta)
            if tlast is not None:
                wt(PE, tlast)
            tp = inc(PE.matmul(PS[:, 3072:3584], onesP[rr:rr + 1, :],
                               rr_ap[rr:rr + 1, :], start=True, stop=True), s_pe)
            wt(DVE, tp)
            tlast = inc(DVE.tensor_copy(r_bc[:, 512 * g:512 * (g + 1)],
                                        PS[:, 3072:3584]), s_dve)
        return tlast


    rloc = nc.partition_id() % 4
    tgi = gpdma(cc_in.ap().rearrange("(a p) t -> p a t", p=128), o_all[:, :, :])
    wt(GP, tgi)
    if SIM_REDUCE:
        for r4 in range(4):
            tcc = gpdma(cc_out.ap()[256 * r4:256 * (r4 + 1), :], cc_in.ap())
        GP.wait_ge(s_gp, cnt[s_gp])
    else:
        tcc = inc(GP.collective_compute(
            "AllGather", ALU.bypass,
            replica_groups=[[0, 1, 2, 3], [4, 5, 6, 7]],
            ins=[cc_in.ap().opt()], outs=[cc_out.ap().opt()]), s_cc)
    d_xin_war = [None, None]
    d_xln_war = [None, None]
    tok_xnev2 = [None]
    d_xdma = [None, None]
    d_xdma[0] = ln_dma(xo_d[0:128, :], 0, None, xin_d)
    for tix in range(16):
        if tix + 1 < 16:
            slp = (tix + 1) % 2
            d_xdma[slp] = ln_dma(xo_d[128 * (tix + 1):128 * (tix + 2), :], slp,
                                 d_xin_war[slp], xin_d)
        sl = tix % 2
        t_xln = emit_ln_tile(xo_d[128 * tix:128 * (tix + 1), :], sl,
                             d_xin_war[sl], d_xln_war[sl], xin_d, xln_d,
                             bn_d[:, 0:32].rearrange("p n -> p n"), d_xdma[sl])
        d_xin_war[sl] = t_xln
        pbx = bankb(3)
        wt(PE, t_xln)
        if tok_xnev2[0] is not None:
            wt(PE, tok_xnev2[0])
        ttp = None
        for c in range(8):
            ttp = inc(PE.transpose(pbx[:, 128 * c:128 * (c + 1)],
                                   xln_d[:, sl, 128 * c:128 * (c + 1)],
                                   identB[:, :]), s_pe)
        d_xln_war[sl] = ttp
        wt(DVE, ttp)
        tok_xnev2[0] = inc(DVE.tensor_copy(
            xn_own[:, :, 128 * tix:128 * (tix + 1)],
            pbx[:, :].rearrange("p (c n) -> p c n", c=8)), s_dve)

    wt(DVE, tok_xnev2[0])
    t_r1 = rms_to_R(lambda ci: xn_own[:, ci, :], 8, 1.0 / D, sq_a, rr1,
                    dep=tok_xnev2[0])
    t_xns = None
    for dc in range(8):
        wt(DVE, t_r1)
        t_xns = inc(DVE.tensor_tensor(xn_own[:, dc, :], xn_own[:, dc, :],
                                      r_bc[:, :], op=ALU.mult), s_dve)

    wt(SP, tcc)
    t_ro = spdma(o_own[:, :, :],
                 cc_out.ap().rearrange("(ic p) t -> p ic t", p=128)
                 [:, :, bass.ts(rloc, NO)])
    t_woc = None
    for hh in range(2):
        if t_woc is not None:
            wt(GP, t_woc)
        twf = gpdma(wout_scr[:, :, :],
                    wo_d.rearrange("(ic p) d -> p ic d", p=128)[:, 4 * hh:4 * (hh + 1), :])
        wt(DVE, twf)
        t_woc = inc(DVE.tensor_copy(wout_bf[:, 4 * hh:4 * (hh + 1), :],
                                    wout_scr[:, :, :]), s_dve)
    wt(PE, t_woc)
    wt(PE, t_ro)
    SP.wait_ge(s_sp, cnt[s_sp])
    tok_sev = {}
    for tg in range(4):
        tsl = slice(512 * tg, 512 * (tg + 1))
        for dc in range(8):
            i = tg * 8 + dc
            pb = i % 2
            sps = PS[:, 512 * pb:512 * (pb + 1)]
            if (i - 2) in tok_sev:
                wt(PE, tok_sev[i - 2])
            tm = None
            for ic in range(8):
                tm = inc(PE.matmul(sps, wout_bf[:, ic, 128 * dc:128 * (dc + 1)],
                                   o_own[:, ic, tsl], start=(ic == 0), stop=(ic == 7)),
                         s_pe)
            wt(ACT, tm)
            wt(ACT, t_bosb)
            tok_sev[i] = inc(
                ACT.activation(s_bf[:, dc, tsl], sps, AF.Identity,
                               bias=bo_sb[:, dc:dc + 1]), s_act)

    nc.all_engine_barrier()

    if DEBUG:
        gpdma(dbg["s"].rearrange("p (a n) -> p a n", a=8), s_bf[:, :, 0:512])
        GP.wait_ge(s_gp, cnt[s_gp])
        nc.all_engine_barrier()
    if PHASES <= 5:
        return nc, dbg

    tok_upmm = {}
    tok_hev = {}
    for fc in range(32):
        slw = fc % 2
        if fc < 2:
            wt(GP, t_r1)
        if fc >= 2:
            wt(GP, tok_upmm[fc - 2])
        twl = gpdma(wbufu[:, slw, :, :],
                    wuT_d[fc, :, :].rearrange("p (c n) -> p c n", c=8), sem=su[slw])
        for g in range(4):
            i = fc * 4 + g
            pb = i % 2
            sps = PS[:, 512 * pb:512 * (pb + 1)]
            wt(PE, twl)
            wt(PE, t_xns)
            if (i - 2) in tok_hev:
                wt(PE, tok_hev[i - 2])
            tm = None
            for dc in range(8):
                tm = inc(PE.matmul(sps, wbufu[:, slw, dc, :],
                                   xn_own[:, dc, 512 * g:512 * (g + 1)],
                                   start=(dc == 0), stop=(dc == 7)), s_pe)
            tok_upmm[fc] = tm
            wt(ACT, tm)
            tok_hev[i] = inc(
                ACT.activation(h_fm[:, fc, 512 * g:512 * (g + 1)], sps, AF.Gelu),
                s_act)

    t_r2 = rms_to_R(lambda ci: h_fm[:, ci, :], 32, 1.0 / FF, sq_b, rr2,
                    dep=tok_hev[127])

    tok_gev = {}
    tok_dnh = {}
    t_sg_last = None
    wbufd_h = wbufd[:, :, :].rearrange("p (hh c) n -> p hh c n", hh=2)
    for dc in range(8):
        if dc == 0:
            wt(GP, tok_upmm[31])
        if (dc - 1, 0) in tok_dnh:
            wt(GP, tok_dnh[(dc - 1, 0)])
        twl0 = gpdma(wbufd_h[:, 0, :, :],
                     wdT_d[dc, :, 0:2048].rearrange("p (c n) -> p c n", c=16),
                     sem=su[0])
        if (dc - 1, 1) in tok_dnh:
            wt(GP, tok_dnh[(dc - 1, 1)])
        twl1 = gpdma(wbufd_h[:, 1, :, :],
                     wdT_d[dc, :, 2048:4096].rearrange("p (c n) -> p c n", c=16),
                     sem=su[1])
        for g in range(4):
            i = dc * 4 + g
            pb = i % 2
            sps = PS[:, 512 * pb:512 * (pb + 1)]
            wt(PE, twl0)
            wt(PE, t_r2)
            if (i - 2) in tok_gev:
                wt(PE, tok_gev[i - 2])
            tm = None
            for fc in range(16):
                tm = inc(PE.matmul(sps, wbufd_h[:, 0, fc, :],
                                   h_fm[:, fc, 512 * g:512 * (g + 1)],
                                   start=(fc == 0), stop=False), s_pe)
            tok_dnh[(dc, 0)] = tm
            wt(PE, twl1)
            for fc in range(16, 32):
                tm = inc(PE.matmul(sps, wbufd_h[:, 1, fc - 16, :],
                                   h_fm[:, fc, 512 * g:512 * (g + 1)],
                                   start=False, stop=(fc == 31)), s_pe)
            tok_dnh[(dc, 1)] = tm
            wt(DVE, tm)
            tgp = inc(DVE.tensor_tensor(gpre[:, pb, :], sps,
                                        r_bc[:, 512 * g:512 * (g + 1)], op=ALU.mult),
                      s_dve)
            wt(ACT, tgp)
            tsi = inc(ACT.activation(gpre[:, pb, :], gpre[:, pb, :], AF.Silu), s_act)
            wt(DVE, tsi)
            tok_gev[i] = t_sg_last = inc(
                DVE.tensor_tensor(s_bf[:, dc, 512 * g:512 * (g + 1)],
                                  gpre[:, pb, :],
                                  s_bf[:, dc, 512 * g:512 * (g + 1)], op=ALU.mult),
                s_dve)

    nc.all_engine_barrier()

    tok_add = [None, None]
    tok_store = [None, None]
    for tix in range(16):
        sl = tix % 2
        if tok_store[sl] is not None:
            wt(SP, tok_store[sl])
        t_x = spdma(xo_ring[:, sl, :], xo_d[128 * tix:128 * (tix + 1), :], sem=so[sl])
        pbf = bankb(3 if sl == 0 else 5)
        if tok_add[sl] is not None:
            wt(PE, tok_add[sl])
        ttp = None
        for c in range(8):
            ttp = inc(PE.transpose(pbf[:, 128 * c:128 * (c + 1)],
                                   s_bf[:, c, 128 * tix:128 * (tix + 1)],
                                   identB[:, :]), s_pe)
        wt(DVE, ttp)
        wt(DVE, t_x)
        tok_add[sl] = inc(DVE.tensor_tensor(xo_ring[:, sl, :], pbf[:, :],
                                            xo_ring[:, sl, :], op=ALU.add), s_dve)
        wt(SP, tok_add[sl])
        tok_store[sl] = spdma(out_d[128 * tix:128 * (tix + 1), :], xo_ring[:, sl, :],
                              sem=so[sl])

    SP.wait_ge(so[0], cnt[so[0]])
    SP.wait_ge(so[1], cnt[so[1]])
    nc.all_engine_barrier()
    return nc, dbg



def kernel(L0_attn_x, L0_attn_w_qkv, L0_attn_w_out, L0_attn_b_out,
           L0_ffn_w_up, L0_ffn_w_down):
    x = np.ascontiguousarray(np.asarray(L0_attn_x, dtype=np.float32))
    wqkv = np.ascontiguousarray(np.asarray(L0_attn_w_qkv, dtype=np.float32))
    wo = np.ascontiguousarray(np.asarray(L0_attn_w_out, dtype=np.float32))
    bo = np.ascontiguousarray(np.asarray(L0_attn_b_out, dtype=np.float32)).reshape(8, 128)
    wu = np.ascontiguousarray(np.asarray(L0_ffn_w_up, dtype=np.float32))
    wd = np.ascontiguousarray(np.asarray(L0_ffn_w_down, dtype=np.float32))

    nc, _ = build_program()
    in_maps = []
    for c in range(8):
        b = c // 4
        r = c % 4
        h0 = 4 * r
        in_maps.append({
            "x": np.ascontiguousarray(x[b]),
            "xo": np.ascontiguousarray(x[b, NO * r:NO * (r + 1)]),
            "wq": np.ascontiguousarray(wqkv[:, 64 * h0:64 * h0 + 256]),
            "wk": np.ascontiguousarray(wqkv[:, 1024 + 64 * h0:1024 + 64 * h0 + 256]),
            "wv": np.ascontiguousarray(wqkv[:, 2048 + 64 * h0:2048 + 64 * h0 + 256]),
            "wo": wo,
            "bo": bo,
            "wu": wu,
            "wd": wd,
        })
    res = run_bass_kernel_spmd(nc, in_maps, core_ids=list(range(8)))
    outs = [res.results[c]["out"] for c in range(8)]
    full = np.concatenate(outs, axis=0).reshape(2, N, D)
    return full
